# version 14
# baseline (speedup 1.0000x reference)
import math
from contextlib import ExitStack

import numpy as np
import ml_dtypes

import concourse.bass as bass
import concourse.mybir as mybir
from concourse.bass_utils import run_bass_kernel_spmd
from concourse import dve_ops as _dve_ops
from concourse.dve_op import DveOp as _DveOp
from concourse.dve_spec import (Spec as _Spec, Src0 as _S0, Src1 as _S1,
                                C0 as _C0, C1 as _C1, C2 as _C2, One as _One,
                                Zero as _Zero, select as _sel, eq as _eq,
                                lower as _lower)


def _lut2_ref(in0, in1, s0, s1, imm2):
    r = np.where(np.asarray(in0) == imm2, s0,
                 np.where(np.asarray(in0) == imm2 + 1.0, s1, in1))
    return r.astype(np.asarray(in1).dtype)


def _register_lut2():
    if any(o.name == "LUT2_SEL" for o in _dve_ops.OPS):
        return next(o for o in _dve_ops.OPS if o.name == "LUT2_SEL")
    spec = _Spec(body=(_S1 + _eq(_S0, _C2) * _C0) +
                 _eq(_S0 - _C2, _One) * _C1)
    shas = {}
    for ver in ("v3",):
        shas[ver] = _lower(spec, ver=ver, perf_en=False,
                           name="LUT2_SEL").sha(ver)
    op = _DveOp("LUT2_SEL", spec, subdim=False, uops_sha=shas,
                reference=_lut2_ref)
    row = max(_dve_ops._SUB_OPCODE_FOR_NAME.values()) + 1
    assert row < 0x20, row
    _dve_ops.OPS.append(op)
    _dve_ops._SUB_OPCODE_FOR_NAME["LUT2_SEL"] = row
    _dve_ops.CUSTOM_DVE_SPECS["LUT2_SEL"] = spec
    _dve_ops.CUSTOM_DVE_REFS["LUT2_SEL"] = _lut2_ref
    return op


_LUT2 = _register_lut2()

F32 = mybir.dt.float32
BF16 = mybir.dt.bfloat16
I32 = mybir.dt.int32
I8 = mybir.dt.int8
AF = mybir.ActivationFunctionType
ALU = mybir.AluOpType
AXX = mybir.AxisListType.X

S = 2048
E = 2048
H = 16
G = 4
D = 128
PER = H // G
INTER = 5504
OC = INTER // 128
EC = E // 128
NT = S // 128
BPC = 2
NCORES = 8
CINV = 1.0 / math.sqrt(D)
LNEPS = 1e-5
NEG = -1.0e30

ENGS = ("sync", "vector", "scalar", "tensor", "gpsimd")
QSEM = {"sync": "ld", "gpsimd": "st"}


class Prog:

    def __init__(self, nc, sems):
        self.nc = nc
        self.sems = sems
        self.cnt = {k: 0 for k in sems}
        self.ops = {e: [] for e in ENGS}
        self.last = {e: None for e in ENGS}
        self.dirty = {e: False for e in ENGS}

    def emit(self, eng, build=None, waits=(), incs=()):
        for s, n in incs:
            self.cnt[s] += n
        rec = [build, list(waits), list(incs)]
        self.ops[eng].append(rec)
        if build is not None:
            self.last[eng] = rec
            self.dirty[eng] = True
        return rec

    def dma(self, q, sem, out, in_, waits=()):
        self.emit(q, lambda p, o=out, i=in_: p.dma_start(out=o, in_=i),
                  waits, [(sem, 16)])
        return self.cnt[sem]

    def add_inc(self, rec, sem, n=1):
        for k, (s0, n0) in enumerate(rec[2]):
            if s0 == sem:
                rec[2][k] = (s0, n0 + n)
                self.cnt[sem] += n
                return
        assert not rec[2], f"instruction already updates {rec[2][0][0]}"
        rec[2].append((sem, n))
        self.cnt[sem] += n

    def barrier(self):
        for e, s in (("vector", "dv"), ("scalar", "ac"), ("tensor", "pe")):
            if self.dirty[e]:
                if not self.last[e][2]:
                    self.add_inc(self.last[e], s, 1)
        tgt = {s: self.cnt[s] for s in self.sems}
        own = {"vector": "dv", "scalar": "ac", "tensor": "pe"}
        for e in ENGS:
            waits = [(s, v) for s, v in tgt.items()
                     if s != own.get(e) and v > 0]
            self.emit(e, None, waits=waits)
        for e in ENGS:
            self.dirty[e] = False


def build(nc, lutvals, taps=(), stop_after=None, debug_outputs=False):
    dt = nc.dram_tensor
    DBG = "ExternalOutput" if debug_outputs else "Internal"
    x_d = dt("x", [BPC, S, E], F32, kind="ExternalInput").ap()
    wq_d = dt("wq", [H, E, D], F32, kind="ExternalInput").ap()
    bq_d = dt("bq", [H, D], F32, kind="ExternalInput").ap()
    wk_d = dt("wk", [G, E, D], F32, kind="ExternalInput").ap()
    bk_d = dt("bk", [G, D], F32, kind="ExternalInput").ap()
    wv_d = dt("wv", [G, E, D], F32, kind="ExternalInput").ap()
    bv_d = dt("bv", [G, D], F32, kind="ExternalInput").ap()
    gidxT_d = dt("gidxT", [E, INTER], I8, kind="ExternalInput").ap()
    uidxT_d = dt("uidxT", [E, INTER], I8, kind="ExternalInput").ap()
    didxT_d = dt("didxT", [INTER, E], I8, kind="ExternalInput").ap()
    ident_d = dt("ident", [128, 128], BF16, kind="ExternalInput").ap()
    maskq_d = dt("maskq", [128, 128], BF16, kind="ExternalInput").ap()
    maskt_d = dt("maskt", [128, 128], BF16, kind="ExternalInput").ap()
    y_d = dt("y", [BPC, S, E], F32, kind="ExternalOutput").ap()
    wg_s = dt("wg_s", [EC, 128, INTER], BF16, kind=DBG).ap()
    wu_s = dt("wu_s", [EC, 128, INTER], BF16, kind=DBG).ap()
    wd_s = dt("wd_s", [OC, 128, E], BF16, kind=DBG).ap()
    x2_s = dt("x2_s", [BPC, S, E], F32, kind=DBG).ap()
    h2_s = dt("h2_s", [BPC, EC, 128, S], BF16, kind=DBG).ap()
    o_s = dt("o_s", [BPC, H, 128, S], BF16, kind=DBG).ap()
    mr_s = dt("mr_s", [16, 128], BF16, kind="Internal").ap()
    if debug_outputs:
        t_qT = dt("t_qT", [128, S], BF16, kind="ExternalOutput").ap()
        t_kT = dt("t_kT", [128, S], BF16, kind="ExternalOutput").ap()
        t_v = dt("t_v", [128, S], BF16, kind="ExternalOutput").ap()
        t_mrow = dt("t_mrow", [1, S], BF16, kind="ExternalOutput").ap()
        t_mcol = dt("t_mcol", [128, 16], BF16, kind="ExternalOutput").ap()
        t_l = dt("t_l", [128, 16], F32, kind="ExternalOutput").ap()
        t_pT = dt("t_pT", [128, S], BF16, kind="ExternalOutput").ap()
        t_hT = dt("t_hT", [128, S], BF16, kind="ExternalOutput").ap()
        t_hTf = dt("t_hTf", [128, EC * S], BF16,
                   kind="ExternalOutput").ap()
        t_s1 = dt("t_s1", [128, 16], F32, kind="ExternalOutput").ap()
        t_ss = dt("t_ss", [128, 16], F32, kind="ExternalOutput").ap()
        t_var = dt("t_var", [128, 16], F32, kind="ExternalOutput").ap()
        t_rstd = dt("t_rstd", [128, 16], F32, kind="ExternalOutput").ap()

    tap_d = {}
    for t, shape, dtt in taps:
        tap_d[t] = dt("tap_" + t, shape, dtt, kind="ExternalOutput").ap()

    with ExitStack() as ctx:
        en = ctx.enter_context

        SBASE = 16896

        def sb(name, cols, dtyp, off):
            return nc.alloc_sbuf_tensor_at(name, [128, cols], dtyp,
                                           offset=SBASE + off).ap()

        maskq = sb("maskq_sb", 128, BF16, 0)
        maskt = sb("maskt_sb", 128, BF16, 512)
        ident = sb("ident_sb", 128, BF16, 1024)
        bq_all = sb("bq_all", H, F32, 1280)
        bk_all = sb("bk_all", G, F32, 1344)
        bv_all = sb("bv_all", G, F32, 1376)
        stats = sb("stats", 16, F32, 1408)
        mcolb = sb("mcolb", 18, BF16, 1472)
        l_all = sb("l_all", 16, F32, 9088)
        accg = sb("accg", 688, BF16, 9152)
        termg = sb("termg", 688, BF16, 10528)
        lrec = sb("lrec", 256, F32, 1568)
        b1a = sb("b1a", 16, F32, 7232)
        c_cinv = sb("c_cinv", 1, F32, 7296)
        dbg4 = sb("dbg4", 64, F32, 7424)
        onesb = nc.alloc_sbuf_tensor_at("onesb", [1, 128], BF16,
                                        offset=SBASE + 2592).ap()
        mrowT = nc.alloc_sbuf_tensor_at("mrowT", [16, 128], BF16,
                                        offset=SBASE + 2848).ap()
        mrow = nc.alloc_sbuf_tensor_at("mrow", [1, S], BF16,
                                       offset=SBASE + 3104).ap()

        B1 = 12 * 1024
        B2 = B1 + 65536
        WK = B2 + 65536
        WKE = WK + 61440
        assert SBASE + WKE <= 224 * 1024

        hT = sb("hT", EC * S, BF16, B1)
        pT = sb("pT", NT * S, BF16, B2)
        hh = sb("hh", OC * 1024, BF16, B1)
        h2Tf = sb("h2Tf", EC * 1024, BF16, B1 + 88064)
        dna = sb("dna", EC * 1024, BF16, B1 + 88064)


        wstg2 = [sb(f"wstg2{j}", 512, F32, WK + j * 2048) for j in (0, 1)]
        wq_b = sb("wq_b", E, BF16, WK + 4096)
        wk_b = sb("wk_b", E, BF16, WK + 8192)
        wv_b = sb("wv_b", E, BF16, WK + 12288)
        qT = sb("qT", S, BF16, WK + 16384)
        kT = sb("kT", S, BF16, WK + 20480)
        vt_b = sb("vt_b", S, BF16, WK + 24576)
        v_sb = sb("v_sb", S, BF16, WK + 28672)
        expsc = sb("expsc", 1024, BF16, WK + 32768)
        oT_sb = sb("oT_sb", S, BF16, WK + 34816)
        WSLICE = 688
        idxh = [sb(f"idxh{j}", WSLICE, I8, WK + 38912 + j * 704)
                for j in (0, 1)]
        idxbf = sb("idxbf", WSLICE, BF16, WK + 40320)
        accw = [sb(f"accw{j}", WSLICE, BF16, WK + 41728 + j * 1408)
                for j in (0, 1)]
        termw = sb("termw", WSLICE, BF16, 7680)
        xt = [sb(f"xt{j}", E, F32, WK + 45056 + j * 8192) for j in (0, 1)]
        h2b = sb("h2b", E, BF16, WK + 20480)
        h2a = sb("h2a", E, BF16, WK + 24576)
        otin = sb("otin", E, BF16, WK + 28672)
        wgt = [sb(f"wgt{j}", EC * 128, BF16, WK + j * 4096) for j in (0, 1)]
        wut = [sb(f"wut{j}", EC * 128, BF16, WK + 8192 + j * 4096)
               for j in (0, 1)]
        wdt = [sb(f"wdt{j}", 2816, BF16, WK + 16384 + j * 5632)
               for j in (0, 1)]
        sil = sb("sil", 1536, BF16, WK + 27648)

        ps_f = en(nc.psum_tensor("ps_f", [128, 3072], F32)).ap()
        ps_b = en(nc.psum_tensor("ps_b", [128, 2048], BF16)).ap()
        RA = ps_f[:, 0:2048]

        sems = {}
        for s in ("dv", "ac", "pe",
                  "ld0", "ldi0", "ldi1", "ldx0", "ldx1", "ldo", "ldw0",
                  "ldm", "ldh", "ldg0", "ldg1", "ldu0", "ldu1", "ldw1",
                  "ldd0", "ldd1",
                  "sta0", "sta1", "sto", "sgx0", "sgx1", "sth", "gp"):
            sems[s] = en(nc.semaphore(s))
        P = Prog(nc, sems)

        guards = {}

        def guard_wait(key):
            return guards.pop(key, [])

        def guard_set(key, sem):
            guards.setdefault(key, []).append((sem, P.cnt[sem]))

        P.dma("sync", "ld0", maskq, maskq_d)
        P.dma("sync", "ld0", maskt, maskt_d)
        P.dma("sync", "ld0", ident, ident_d)
        for hh_ in range(H):
            P.dma("sync", "ld0", bq_all[:, hh_:hh_ + 1],
                  bq_d[hh_].rearrange("(d o) -> d o", o=1))
        for g in range(G):
            P.dma("sync", "ld0", bk_all[:, g:g + 1],
                  bk_d[g].rearrange("(d o) -> d o", o=1))
            P.dma("sync", "ld0", bv_all[:, g:g + 1],
                  bv_d[g].rearrange("(d o) -> d o", o=1))
        P.emit("vector", lambda p: p.memset(onesb[:], 1.0), [], [("dv", 1)])
        P.barrier()

        wsteps = []
        for hc in range(EC):
            for c0 in range(0, INTER, WSLICE):
                cw = min(WSLICE, INTER - c0)
                wsteps.append((gidxT_d[hc * 128:(hc + 1) * 128, c0:c0 + cw],
                               wg_s[hc][:, c0:c0 + cw], cw))
        for hc in range(EC):
            for c0 in range(0, INTER, WSLICE):
                cw = min(WSLICE, INTER - c0)
                wsteps.append((uidxT_d[hc * 128:(hc + 1) * 128, c0:c0 + cw],
                               wu_s[hc][:, c0:c0 + cw], cw))
        for oc in range(OC):
            for c0 in range(0, E, WSLICE):
                cw = min(WSLICE, E - c0)
                wsteps.append((didxT_d[oc * 128:(oc + 1) * 128, c0:c0 + cw],
                               wd_s[oc][:, c0:c0 + cw], cw))
        N_GU = 2 * (INTER + WSLICE - 1) // WSLICE * EC
        wS = {"t": 0, "dv": [0] * len(wsteps), "st": [0] * len(wsteps),
              "ld": [0] * len(wsteps), "carry": 0.0}

        def w_pace(limit, ration):
            wS["carry"] += ration
            n = int(wS["carry"])
            wS["carry"] -= n
            while n > 0 and wS["t"] < limit:
                w_step_one()
                n -= 1

        def w_step(n, limit=None):
            if limit is None:
                limit = len(wsteps)
            for _ in range(n):
                if wS["t"] >= limit:
                    return
                w_step_one()

        def w_step_one():
                t = wS["t"]
                wS["t"] = t + 1
                src, dst, w = wsteps[t]
                ib = idxh[t % 2][:, 0:w]
                ab = accw[t % 2][:, 0:w]
                waits = [("dv", wS["dv"][t - 2])] if t >= 2 else []
                wS["ld"][t] = P.dma("sync", f"ldi{t % 2}", ib, src,
                                    waits=waits)
                rec = P.emit("gpsimd",
                             lambda p, i=ib, w_=w: p.tensor_copy(
                                 out=idxbf[:, 0:w_], in_=i),
                             [(f"ldi{t % 2}", wS["ld"][t])] +
                             ([("dv", wS["dv"][t - 1])] if t >= 1 else []) +
                             ([(f"sta{t % 2}", wS["st"][t - 2])]
                              if t >= 2 else []))
                P.add_inc(rec, "gp", 1)
                gpc = P.cnt["gp"]
                P.emit("gpsimd",
                       lambda p, w_=w: p.tensor_scalar(
                           out=accg[:, 0:w_], in0=idxbf[:, 0:w_],
                           scalar1=14.0, scalar2=lutvals[14],
                           op0=ALU.is_equal, op1=ALU.mult))
                P.emit("gpsimd",
                       lambda p, w_=w: p.tensor_scalar(
                           out=termg[:, 0:w_], in0=idxbf[:, 0:w_],
                           scalar1=15.0, scalar2=lutvals[15],
                           op0=ALU.is_equal, op1=ALU.mult))
                rec = P.emit("gpsimd",
                             lambda p, w_=w: p.tensor_tensor(
                                 out=accg[:, 0:w_], in0=accg[:, 0:w_],
                                 in1=termg[:, 0:w_], op=ALU.add))
                P.add_inc(rec, "gp", 1)
                P.emit("vector",
                       lambda p, a=ab, w_=w: p.tensor_scalar(
                           out=a, in0=idxbf[:, 0:w_], scalar1=0.0,
                           scalar2=lutvals[0], op0=ALU.is_equal,
                           op1=ALU.mult),
                       waits=[("gp", gpc)])
                for tv in range(1, 14):
                    P.emit("vector",
                           lambda p, w_=w, tv_=tv: p.tensor_scalar(
                               out=termw[:, 0:w_], in0=idxbf[:, 0:w_],
                               scalar1=float(tv_), scalar2=lutvals[tv_],
                               op0=ALU.is_equal, op1=ALU.mult))
                    P.emit("vector",
                           lambda p, a=ab, w_=w: p.tensor_tensor(
                               out=a, in0=a, in1=termw[:, 0:w_],
                               op=ALU.add))
                rec = P.emit("vector",
                             lambda p, a=ab, w_=w: p.tensor_tensor(
                                 out=a, in0=a, in1=accg[:, 0:w_],
                                 op=ALU.add),
                             waits=[("gp", gpc + 1)])
                P.add_inc(rec, "dv", 1)
                wS["dv"][t] = P.cnt["dv"]
                wS["st"][t] = P.dma("gpsimd", f"sta{t % 2}", dst, ab,
                                    waits=[("dv", wS["dv"][t])])

        s1c = stats[:, 0:1]
        nmc = stats[:, 1:2]
        ssc = stats[:, 2:3]
        varc = stats[:, 3:4]
        rstd = stats[:, 4:5]
        sqsc = stats[:, 5:6]
        mf32 = stats[:, 6:7]
        b1c = stats[:, 7:8]

        def ln_tile(xtile, out_bf, sq_out, dbgi=None):
            P.ops["vector"].append([None, [("pe", P.cnt["pe"])], []])
            rec = P.emit("vector", lambda p: p.reduce_sum(out=s1c, in_=xtile,
                                                          axis=AXX))
            P.add_inc(rec, "dv", 1)
            rec = P.emit("vector", lambda p: p.tensor_scalar(
                out=nmc, in0=s1c, scalar1=-1.0 / E, scalar2=None,
                op0=ALU.mult), waits=[("dv", P.cnt["dv"])])
            P.add_inc(rec, "dv", 1)
            dvv = P.cnt["dv"]
            rec = P.emit("scalar", lambda p: p.activation(
                out=sq_out, in_=xtile, func=AF.Square, bias=0.0, scale=1.0,
                accum_out=ssc), waits=[("dv", dvv)])
            P.add_inc(rec, "ac", 1)
            acv = P.cnt["ac"]
            P.emit("vector", lambda p: p.tensor_tensor(
                out=sqsc, in0=nmc, in1=nmc, op=ALU.mult),
                waits=[("ac", acv)])
            rec = P.emit("vector", lambda p: p.tensor_scalar(
                out=varc, in0=ssc, scalar1=1.0 / E, scalar2=LNEPS,
                op0=ALU.mult, op1=ALU.add))
            P.add_inc(rec, "dv", 1)
            rec = P.emit("vector", lambda p: p.tensor_tensor(
                out=varc, in0=varc, in1=sqsc, op=ALU.subtract),
                waits=[("dv", P.cnt["dv"])])
            P.add_inc(rec, "dv", 1)
            dvv = P.cnt["dv"]
            rec = P.emit("scalar", lambda p: p.activation(
                out=sqsc, in_=varc, func=AF.Sqrt, bias=0.0, scale=1.0),
                waits=[("dv", dvv)])
            P.add_inc(rec, "ac", 1)
            acv = P.cnt["ac"]
            rec = P.emit("vector", lambda p: p.reciprocal(out=rstd,
                                                          in_=sqsc),
                         waits=[("ac", acv)])
            P.add_inc(rec, "dv", 1)
            P.ops["vector"].append([None, [("dv", P.cnt["dv"])], []])
            if dbgi is not None:
                P.emit("vector", lambda p, i_=dbgi: p.tensor_copy(
                    out=dbg4[:, i_:i_ + 1], in_=s1c))
                P.emit("vector", lambda p, i_=dbgi: p.tensor_copy(
                    out=dbg4[:, 16 + i_:16 + i_ + 1], in_=ssc))
                P.emit("vector", lambda p, i_=dbgi: p.tensor_copy(
                    out=dbg4[:, 32 + i_:32 + i_ + 1], in_=varc))
                P.emit("vector", lambda p, i_=dbgi: p.tensor_copy(
                    out=dbg4[:, 48 + i_:48 + i_ + 1], in_=rstd))
            rec = P.emit("vector", lambda p: p.tensor_scalar(
                out=out_bf, in0=xtile, scalar1=nmc, scalar2=rstd,
                op0=ALU.add, op1=ALU.mult))
            P.add_inc(rec, "dv", 1)
            return P.cnt["dv"]

        def transpose_groups(src_bf, dst_bf16, nblk, src_off, dst_view,
                             pe_waits=()):
            for j0 in range(0, nblk, 4):
                nb = min(4, nblk - j0)
                half = (j0 // 4) % 2
                pw = list(pe_waits) + guard_wait(("tp", half))
                for j in range(nb):
                    rec = P.emit("tensor", lambda p, a=src_off + (j0 + j) * 128,
                                 hf=half, jj=j: p.transpose(
                                     out=ps_b[:, hf * 1024 + jj * 128:
                                              hf * 1024 + (jj + 1) * 128],
                                     in_=src_bf[:, a:a + 128],
                                     identity=ident[:]),
                                 waits=pw if j == 0 else [])
                P.add_inc(rec, "pe", 1)
                pev = P.cnt["pe"]
                rec = P.emit("vector", lambda p, hf=half, n=nb, j0_=j0:
                             p.tensor_copy(
                                 out=dst_view(j0_, n),
                                 in_=ps_b[:, hf * 1024:hf * 1024 + n * 128]
                                 .rearrange("p (a c) -> p a c", c=128)),
                             waits=[("pe", pev)])
                P.add_inc(rec, "dv", 1)
                guard_set(("tp", half), "dv")

        xld = {}

        def load_x_tile(b, i, which, src=x_d):
            waits = []
            key = ("xt", which)
            if key in xld:
                waits = xld[key]
            v = P.dma("sync", f"ldx{which}", xt[which][:],
                      src[b, i * 128:(i + 1) * 128, :], waits=waits)
            xld[key + ("ld",)] = v
            return v

        def x_tile_consumed(which, marks):
            xld[("xt", which)] = marks

        def p1(b):
            for i in range(NT):
                w = i % 2
                ldv = load_x_tile(b, i, w)
                P.ops["vector"].append([None, [(f"ldx{w}", ldv)], []])
                dvv = ln_tile(xt[w][:], h2b[:], h2b[:],
                              dbgi=(i if debug_outputs and b == 0 else None))
                def dv_hT(j0, n, i=i):
                    return hT[:, :].rearrange("p (ec t) -> p ec t", t=S)[
                        :, j0:j0 + n, i * 128:(i + 1) * 128]
                import os as _os
                if not _os.environ.get("K_NO_TRANS"):
                    transpose_groups(h2b, None, EC, 0, dv_hT,
                                     pe_waits=[("dv", dvv)])
                x_tile_consumed(w, [("dv", dvv), ("ac", P.cnt["ac"])])
                w_pace(N_GU, 0.22)

        slot_ctr = [0]

        def next_slot():
            sl = slot_ctr[0] % 3
            slot_ctr[0] += 1
            return sl

        def project(w_bf, bias_ap, out_sb, tag, scale=None):
            for pc in range(4):
                sl = next_slot()
                base = sl * 1024
                pw = guard_wait(("sl", sl))
                for ec in range(EC):
                    rec = P.emit("tensor", lambda p, e=ec, pc_=pc, b_=base:
                                 p.matmul(
                                     out=ps_f[:, b_:b_ + 512],
                                     lhsT=w_bf[:, e * 128:(e + 1) * 128],
                                     rhs=hT[:, e * S + pc_ * 512:
                                            e * S + pc_ * 512 + 512],
                                     start=(e == 0), stop=(e == EC - 1)),
                                 waits=pw if ec == 0 else [])
                P.add_inc(rec, "pe", 1)
                pev = P.cnt["pe"]
                if scale is None:
                    rec = P.emit("scalar", lambda p, pc_=pc, b_=base:
                                 p.activation(
                                     out=out_sb[:, pc_ * 512:
                                                pc_ * 512 + 512],
                                     in_=ps_f[:, b_:b_ + 512],
                                     func=AF.Identity, bias=bias_ap,
                                     scale=1.0),
                                 waits=[("pe", pev)])
                    P.add_inc(rec, "ac", 1)
                    guard_set(("sl", sl), "ac")
                else:
                    rec = P.emit("vector", lambda p, pc_=pc, b_=base,
                                 sc_=scale: p.tensor_scalar(
                                     out=out_sb[:, pc_ * 512:
                                                pc_ * 512 + 512],
                                     in0=ps_f[:, b_:b_ + 512],
                                     scalar1=bias_ap, scalar2=sc_,
                                     op0=ALU.add, op1=ALU.mult),
                                 waits=[("pe", pev)])
                    P.add_inc(rec, "dv", 1)
                    guard_set(("sl", sl), "dv")
            return P.cnt["ac"], P.cnt["pe"]

        wcast = {"j": 0}

        def load_cast_w(src_ap, dst_bf, tag):
            src = src_ap.rearrange("(ec p) d -> p ec d", p=128)
            for j in range(4):
                wb = wcast["j"] % 2
                wcast["j"] += 1
                ldv = P.dma("sync", f"ldw{wb}",
                            wstg2[wb][:, :].rearrange("p (e d) -> p e d",
                                                      d=128),
                            src[:, 4 * j:4 * j + 4, :],
                            waits=[("dv", wcast.get(("c", wb), 0))])
                rec = P.emit("vector",
                             lambda p, d=dst_bf, j_=j, w_=wb: p.tensor_copy(
                                 out=d[:, j_ * 512:(j_ + 1) * 512],
                                 in_=wstg2[w_][:]),
                             waits=[(f"ldw{wb}", ldv),
                                    ("pe", wcast.get(tag, 0) if j == 0
                                     else 0)])
                P.add_inc(rec, "dv", 1)
                wcast[("c", wb)] = P.cnt["dv"]
            return P.cnt["dv"]

        lp0 = stats[:, 8:9]
        lp1 = stats[:, 9:10]
        nmh = stats[:, 10:11]

        def attn(b):
            for g in range(G):
                dvv = load_cast_w(wk_d[g], wk_b, "wk")
                P.ops["tensor"].append([None, [("dv", dvv)], []])
                kdone, pemark = project(wk_b, bk_all[:, g:g + 1], kT, "k")
                wcast["wk"] = pemark
                dvv = load_cast_w(wv_d[g], wv_b, "wv")
                P.ops["tensor"].append([None, [("dv", dvv)], []])
                vdone, pemark = project(wv_b, bv_all[:, g:g + 1], vt_b, "v")
                wcast["wv"] = pemark

                def dv_v(j0, n):
                    return v_sb[:, j0 * 128:(j0 + n) * 128].rearrange(
                        "p (a c) -> p a c", c=128)
                transpose_groups(vt_b, None, NT, 0, dv_v,
                                 pe_waits=[("ac", vdone)])
                for h in range(g * PER, (g + 1) * PER):
                    dvv = load_cast_w(wq_d[h], wq_b, "wq")
                    P.ops["tensor"].append([None, [("dv", dvv)], []])
                    _, pemark = project(wq_b, bq_all[:, h:h + 1], qT,
                                        "q", scale=CINV)
                    wcast["wq"] = pemark
                    for i in range(NT):
                        ext = 128 * (i + 1)
                        nh = (ext + 1023) // 1024
                        tile_slots = []
                        for hf in range(nh):
                            c0 = 1024 * hf
                            cwH = min(1024, ext - c0)
                            sl = next_slot()
                            base = sl * 1024
                            pw = guard_wait(("sl", sl))
                            doff = ext - 128 - c0
                            for j in range((cwH + 511) // 512):
                                cj = j * 512
                                cw = min(512, cwH - cj)
                                hasdiag = (hf == nh - 1 and
                                           cj <= doff < cj + cw)
                                rec = P.emit(
                                    "tensor",
                                    lambda p, i_=i, b_=base + cj,
                                    s_=c0 + cj, c_=cw, sp=(not hasdiag):
                                    p.matmul(
                                        out=ps_f[:, b_:b_ + c_],
                                        lhsT=qT[:, i_ * 128:(i_ + 1) * 128],
                                        rhs=kT[:, s_:s_ + c_],
                                        start=True, stop=sp),
                                    waits=pw if j == 0 else [])
                                if hasdiag:
                                    rec = P.emit(
                                        "tensor",
                                        lambda p, b_=base + doff: p.matmul(
                                            out=ps_f[:, b_:b_ + 128],
                                            lhsT=ident[:], rhs=maskq[:],
                                            start=False, stop=True))
                            P.add_inc(rec, "pe", 1)
                            pev = P.cnt["pe"]
                            tgt = (mcolb[:, i:i + 1] if hf == 0
                                   else mcolb[:, 16:17])
                            rec = P.emit(
                                "vector",
                                lambda p, b_=base, c_=cwH, t_=tgt:
                                p.reduce_max(out=t_, in_=ps_f[:, b_:b_ + c_],
                                             axis=AXX, negate=True),
                                waits=[("pe", pev)])
                            P.add_inc(rec, "dv", 1)
                            if hf > 0:
                                rec = P.emit(
                                    "vector",
                                    lambda p, i_=i: p.tensor_tensor(
                                        out=mcolb[:, i_:i_ + 1],
                                        in0=mcolb[:, i_:i_ + 1],
                                        in1=mcolb[:, 16:17], op=ALU.min),
                                    waits=[("dv", P.cnt["dv"])])
                                P.add_inc(rec, "dv", 1)
                            tile_slots.append((sl, c0, cwH))
                        rec = P.emit("vector", lambda p, i_=i:
                                     p.tensor_copy(
                                         out=b1a[:, i_:i_ + 1],
                                         in_=mcolb[:, i_:i_ + 1]),
                                     waits=[("dv", P.cnt["dv"])])
                        P.add_inc(rec, "dv", 1)
                        b1v = P.cnt["dv"]
                        for hf, (sl, c0, cwH) in enumerate(tile_slots):
                            lp = (l_all[:, i:i + 1] if nh == 1
                                  else (lp0 if hf == 0 else lp1))
                            rec = P.emit(
                                "scalar",
                                lambda p, b_=sl * 1024, c_=cwH, i_=i,
                                lp_=lp: p.activation(
                                    out=expsc[:, 0:c_],
                                    in_=ps_f[:, b_:b_ + c_], func=AF.Exp,
                                    bias=b1a[:, i_:i_ + 1], scale=1.0,
                                    accum_out=lp_),
                                waits=[("dv", b1v)])
                            P.add_inc(rec, "ac", 1)
                            guard_set(("sl", sl), "ac")
                        acv = P.cnt["ac"]
                        if nh > 1:
                            rec = P.emit("vector", lambda p, i_=i:
                                         p.tensor_tensor(
                                             out=l_all[:, i_:i_ + 1],
                                             in0=lp0, in1=lp1, op=ALU.add),
                                         waits=[("ac", acv)])
                            P.add_inc(rec, "dv", 1)
                        w_pace(N_GU, 0.22)
                    rec = P.emit("vector", lambda p, h_=h: p.reciprocal(
                        out=lrec[:, h_ * 16:h_ * 16 + 16], in_=l_all[:]),
                        waits=[("dv", P.cnt["dv"])])
                    P.add_inc(rec, "dv", 1)
                    pw = guard_wait(("tp", 0)) + [("dv", P.cnt["dv"])]
                    rec = P.emit("tensor", lambda p: p.transpose(
                        out=ps_b[0:16, 0:128], in_=mcolb[:, 0:16],
                        identity=ident[:]), waits=pw)
                    P.add_inc(rec, "pe", 1)
                    rec = P.emit("vector", lambda p: p.tensor_copy(
                        out=mrowT[:], in_=ps_b[0:16, 0:128]),
                        waits=[("pe", P.cnt["pe"])])
                    P.add_inc(rec, "dv", 1)
                    guard_set(("tp", 0), "dv")
                    m1 = P.dma("sync", "ldm", mr_s, mrowT,
                               waits=[("dv", P.cnt["dv"])])
                    mld = P.dma("sync", "ldm", mrow,
                                mr_s.rearrange("i p -> (i p)").rearrange(
                                    "(o q) -> o q", o=1),
                                waits=[("ldm", m1)])
                    if debug_outputs and b == 0 and h == 0:
                        P.dma("gpsimd", "sto", t_qT, qT,
                              waits=[("ac", P.cnt["ac"])])
                        P.dma("gpsimd", "sto", t_kT, kT)
                        P.dma("gpsimd", "sto", t_v, v_sb)
                        P.dma("gpsimd", "sto", t_mrow, mrow,
                              waits=[("ldm", mld)])
                        P.dma("gpsimd", "sto", t_mcol, mcolb,
                              waits=[("dv", P.cnt["dv"])])
                        P.dma("gpsimd", "sto", t_l, l_all)
                        P.dma("gpsimd", "sto", t_hT, hT[:, 0:S])
                    for kc in range(NT):
                        w = S - kc * 128
                        for hf in range((w + 1023) // 1024):
                            c0 = kc * 128 + 1024 * hf
                            cwH = min(1024, S - c0)
                            sl = next_slot()
                            base = sl * 1024
                            pw = guard_wait(("sl", sl)) + [("ldm", mld)]
                            for j in range((cwH + 511) // 512):
                                q0 = c0 + j * 512
                                cw = min(512, S - q0)
                                rec = P.emit(
                                    "tensor",
                                    lambda p, kc_=kc, q0_=q0, cw_=cw,
                                    b_=base + j * 512: p.matmul(
                                        out=ps_f[:, b_:b_ + cw_],
                                        lhsT=kT[:, kc_ * 128:
                                                (kc_ + 1) * 128],
                                        rhs=qT[:, q0_:q0_ + cw_],
                                        start=True, stop=False),
                                    waits=pw if j == 0 else [])
                                rec = P.emit(
                                    "tensor",
                                    lambda p, q0_=q0, cw_=cw,
                                    b_=base + j * 512,
                                    sp=not (hf == 0 and j == 0): p.matmul(
                                        out=ps_f[:, b_:b_ + cw_],
                                        lhsT=onesb[:],
                                        rhs=mrow[0:1, q0_:q0_ + cw_],
                                        start=False, stop=sp))
                                if hf == 0 and j == 0:
                                    rec = P.emit(
                                        "tensor",
                                        lambda p, b_=base: p.matmul(
                                            out=ps_f[:, b_:b_ + 128],
                                            lhsT=ident[:], rhs=maskt[:],
                                            start=False, stop=True))
                            P.add_inc(rec, "pe", 1)
                            pev = P.cnt["pe"]
                            ew = [("pe", pev)]
                            rec = P.emit(
                                "scalar",
                                lambda p, kc_=kc, c0_=c0, c_=cwH, b_=base:
                                p.activation(
                                    out=pT[:, kc_ * S + c0_:
                                           kc_ * S + c0_ + c_],
                                    in_=ps_f[:, b_:b_ + c_], func=AF.Exp,
                                    bias=0.0, scale=1.0),
                                waits=ew)
                            P.add_inc(rec, "ac", 1)
                            guard_set(("sl", sl), "ac")
                            w_pace(N_GU, 0.22)
                    if debug_outputs and b == 0 and h == 0:
                        P.dma("gpsimd", "sto", t_pT, pT[:, 0:S],
                              waits=[("ac", P.cnt["ac"])])
                    pexp = P.cnt["ac"]
                    for qc in range(4):
                        sl = next_slot()
                        base = sl * 1024
                        pw = guard_wait(("sl", sl)) + [("ac", pexp)]
                        last_kc = min(NT - 1, 4 * qc + 3)
                        for kc in range(last_kc + 1):
                            qstart = max(qc * 512, kc * 128)
                            cw = qc * 512 + 512 - qstart
                            off = qstart - qc * 512
                            rec = P.emit(
                                "tensor",
                                lambda p, kc_=kc, qs=qstart, c=cw, o=off,
                                b_=base, s0=(kc == 0),
                                s1=(kc == last_kc): p.matmul(
                                    out=ps_f[:, b_ + o:b_ + o + c],
                                    lhsT=v_sb[:, kc_ * 128:(kc_ + 1) * 128],
                                    rhs=pT[:, kc_ * S + qs:
                                           kc_ * S + qs + c],
                                    start=s0, stop=s1),
                                waits=pw if kc == 0 else [])
                        P.add_inc(rec, "pe", 1)
                        pev = P.cnt["pe"]
                        rec = P.emit("scalar", lambda p, qc_=qc, b_=base:
                                     p.activation(
                                         out=oT_sb[:, qc_ * 512:
                                                   qc_ * 512 + 512],
                                         in_=ps_f[:, b_:b_ + 512],
                                         func=AF.Copy, bias=0.0, scale=1.0),
                                     waits=[("pe", pev)])
                        P.add_inc(rec, "ac", 1)
                        guard_set(("sl", sl), "ac")
                    stv = P.dma("gpsimd", "sto", o_s[b, h], oT_sb,
                                waits=[("ac", P.cnt["ac"])])
                    P.ops["scalar"].append([None, [("sto", stv)], []])

        def res(b):
            for i in range(NT):
                w = i % 2
                ldx = load_x_tile(b, i, w)
                ldo = P.dma("sync", "ldo", otin[:, :].rearrange(
                    "p (a c) -> p a c", c=128),
                    o_s[b, :, :, i * 128:(i + 1) * 128].rearrange(
                        "a p c -> p a c"),
                    waits=[("dv", xld.get(("otin",), 0))])
                for j0 in range(0, H, 4):
                    half = (j0 // 4) % 2
                    pw = guard_wait(("tp", half)) + \
                        ([("ldo", ldo)] if j0 == 0 else [])
                    for j in range(4):
                        rec = P.emit("tensor",
                                     lambda p, a=(j0 + j) * 128, hf=half,
                                     jj=j: p.transpose(
                                         out=ps_b[:, hf * 1024 + jj * 128:
                                                  hf * 1024 + (jj + 1) * 128],
                                         in_=otin[:, a:a + 128],
                                         identity=ident[:]),
                                     waits=pw if j == 0 else
                                     ([(f"ldx{w}", ldx)] if j == 1 and
                                      j0 == 0 else []))
                    P.add_inc(rec, "pe", 1)
                    pev = P.cnt["pe"]
                    for j in range(4):
                        hh_ = j0 + j
                        rec = P.emit("vector",
                                     lambda p, h_=hh_, hf=half, jj=j, i_=i,
                                     w_=w: p.scalar_tensor_tensor(
                                         out=xt[w_][:, h_ * 128:
                                                    (h_ + 1) * 128],
                                         in0=ps_b[:, hf * 1024 + jj * 128:
                                                  hf * 1024 + (jj + 1) * 128],
                                         scalar=lrec[:, h_ * 16 + i_:
                                                     h_ * 16 + i_ + 1],
                                         in1=xt[w_][:, h_ * 128:
                                                    (h_ + 1) * 128],
                                         op0=ALU.mult, op1=ALU.add),
                                     waits=[("pe", pev)] if j == 0 else [])
                    P.add_inc(rec, "dv", 1)
                    guard_set(("tp", half), "dv")
                xld[("otin",)] = P.cnt["dv"]
                stx = P.dma("gpsimd", f"sgx{w}",
                            x2_s[b, i * 128:(i + 1) * 128, :],
                            xt[w][:], waits=[("dv", P.cnt["dv"])])
                dvv = ln_tile(xt[w][:], h2b[:], h2b[:])

                def dv_h2(j0, n):
                    return h2a[:, j0 * 128:(j0 + n) * 128].rearrange(
                        "p (a c) -> p a c", c=128)
                h2w = [("sth", xld.get(("h2a",), 0))]
                transpose_groups(h2b, None, EC, 0, dv_h2,
                                 pe_waits=[("dv", dvv)] + h2w)
                sth = P.dma("gpsimd", "sth",
                            h2_s[b, :, :, i * 128:(i + 1) * 128].rearrange(
                                "a p c -> p a c"),
                            h2a[:, :].rearrange("p (a c) -> p a c", c=128),
                            waits=[("dv", P.cnt["dv"])])
                xld[("h2a",)] = sth
                x_tile_consumed(w, [("dv", P.cnt["dv"]),
                                    ("ac", P.cnt["ac"]),
                                    (f"sgx{w}", stx)])
                w_pace(len(wsteps), 2.0)

        def ffn():
            wld = {}
            for sc in range(4):
                b, thalf = divmod(sc, 2)
                t0 = thalf * 1024
                ldh = P.dma("sync", "ldh", h2Tf[:, :].rearrange(
                    "p (ec t) -> p ec t", t=1024),
                    h2_s[b, :, :, t0:t0 + 1024].rearrange(
                        "e p t -> p e t"),
                    waits=[("dv", wld.get("fin_dv", 0)),
                           ("pe", wld.get("fin_pe", 0))])
                for oc in range(OC):
                    wb = oc % 2
                    lw = [("pe", wld.get(("wg", oc - 2), 0))]
                    ldg = P.dma("sync", f"ldg{wb}", wgt[wb][:, :].rearrange(
                        "p (a c) -> p a c", c=128),
                        wg_s[:, :, oc * 128:(oc + 1) * 128].rearrange(
                            "a p c -> p a c"), waits=lw)
                    ldu = P.dma("sync", f"ldu{wb}", wut[wb][:, :].rearrange(
                        "p (a c) -> p a c", c=128),
                        wu_s[:, :, oc * 128:(oc + 1) * 128].rearrange(
                            "a p c -> p a c"))
                    for tc in range(2):
                        sl = next_slot()
                        gslot = ps_f[:, sl * 1024:sl * 1024 + 512]
                        uslot = ps_f[:, sl * 1024 + 512:sl * 1024 + 1024]
                        silsl = sil[:, sl * 512:sl * 512 + 512]
                        pw = guard_wait(("sl", sl)) + \
                            [(f"ldg{wb}", ldg), (f"ldu{wb}", ldu),
                             ("ldh", ldh)]
                        for ec2 in range(EC):
                            P.emit("tensor", lambda p, e=ec2, t=tc, g_=gslot,
                                   w_=wb: p.matmul(
                                       out=g_,
                                       lhsT=wgt[w_][:, e * 128:
                                                    (e + 1) * 128],
                                       rhs=h2Tf[:, e * 1024 + t * 512:
                                                e * 1024 + t * 512 + 512],
                                       start=(e == 0), stop=(e == EC - 1)),
                                   waits=pw if ec2 == 0 else [])
                        for ec2 in range(EC):
                            rec = P.emit("tensor",
                                         lambda p, e=ec2, t=tc, u_=uslot,
                                         w_=wb: p.matmul(
                                             out=u_,
                                             lhsT=wut[w_][:, e * 128:
                                                          (e + 1) * 128],
                                             rhs=h2Tf[:, e * 1024 + t * 512:
                                                      e * 1024 + t * 512
                                                      + 512],
                                             start=(e == 0),
                                             stop=(e == EC - 1)))
                        P.add_inc(rec, "pe", 1)
                        if tc == 1:
                            wld[("wg", oc)] = P.cnt["pe"]
                        pev = P.cnt["pe"]
                        rec = P.emit("scalar", lambda p, g_=gslot,
                                     s_=silsl: p.activation(
                                         out=s_, in_=g_, func=AF.Silu,
                                         bias=0.0, scale=1.0),
                                     waits=[("pe", pev)])
                        P.add_inc(rec, "ac", 1)
                        acv = P.cnt["ac"]
                        rec = P.emit("vector",
                                     lambda p, oc_=oc, t=tc, u_=uslot,
                                     s_=silsl: p.tensor_tensor(
                                         out=hh[:, oc_ * 1024 + t * 512:
                                                oc_ * 1024 + t * 512 + 512],
                                         in0=s_, in1=u_, op=ALU.mult),
                                     waits=[("ac", acv)] +
                                     ([("pe", wld.get("hh_pe", 0))]
                                      if oc == 0 and tc == 0 else []))
                        P.add_inc(rec, "dv", 1)
                        guard_set(("sl", sl), "dv")
                    w_pace(len(wsteps), 2.0)
                hhdone = P.cnt["dv"]
                for hc in range(EC):
                    for half in range(2):
                        oc0 = 0 if half == 0 else 22
                        ntl = 22 if half == 0 else 21
                        assert wS["t"] >= len(wsteps), "wd not materialized"
                        wld[("wd", hc, half)] = P.dma(
                            "sync", f"ldd{half}",
                            wdt[half][:, 0:ntl * 128].rearrange(
                                "p (a c) -> p a c", c=128),
                            wd_s[oc0:oc0 + ntl, :,
                                 hc * 128:(hc + 1) * 128].rearrange(
                                     "a p c -> p a c"),
                            waits=[("pe", wld.get(("wdu", hc - 1, half), 0)),
                                   ("sta0", P.cnt["sta0"]),
                                   ("sta1", P.cnt["sta1"])])
                    for tc in range(2):
                        sl = next_slot()
                        base = sl * 1024
                        pw = guard_wait(("sl", sl)) + \
                            [("dv", hhdone),
                             ("ldd0", wld[("wd", hc, 0)]),
                             ("ldd1", wld[("wd", hc, 1)])]
                        for oc in range(OC):
                            hf, rel = (0, oc) if oc < 22 else (1, oc - 22)
                            rec = P.emit("tensor",
                                         lambda p, o=oc, h_=hf, r=rel, t=tc,
                                         b_=base: p.matmul(
                                             out=ps_f[:, b_:b_ + 512],
                                             lhsT=wdt[h_][:, r * 128:
                                                          (r + 1) * 128],
                                             rhs=hh[:, o * 1024 + t * 512:
                                                    o * 1024 + t * 512
                                                    + 512],
                                             start=(o == 0),
                                             stop=(o == OC - 1)),
                                         waits=pw if oc == 0 else [])
                        P.add_inc(rec, "pe", 1)
                        if tc == 1:
                            wld[("wdu", hc, 0)] = P.cnt["pe"]
                            wld[("wdu", hc, 1)] = P.cnt["pe"]
                        pev = P.cnt["pe"]
                        rec = P.emit("scalar", lambda p, hc_=hc, t=tc,
                                     b_=base: p.activation(
                                         out=dna[:, hc_ * 1024 + t * 512:
                                                 hc_ * 1024 + t * 512 + 512],
                                         in_=ps_f[:, b_:b_ + 512],
                                         func=AF.Copy, bias=0.0, scale=1.0),
                                     waits=[("pe", pev)])
                        P.add_inc(rec, "ac", 1)
                        guard_set(("sl", sl), "ac")
                wld["hh_pe"] = P.cnt["pe"]
                dndone = P.cnt["ac"]
                for tt in range(8):
                    w = tt % 2
                    ldx = P.dma("sync", f"ldx{w}", xt[w][:],
                                x2_s[b, t0 + tt * 128:t0 + (tt + 1) * 128, :],
                                waits=xld.get(("fxt", w), []))
                    for j0 in range(0, EC, 4):
                        half = (j0 // 4) % 2
                        pw = guard_wait(("tp", half)) + \
                            ([("ac", dndone)] if j0 == 0 else [])
                        for j in range(4):
                            rec = P.emit("tensor",
                                         lambda p, hc=j0 + j, hf=half, jj=j,
                                         tt_=tt: p.transpose(
                                             out=ps_b[:, hf * 1024 +
                                                      jj * 128:
                                                      hf * 1024 +
                                                      (jj + 1) * 128],
                                             in_=dna[:, hc * 1024 +
                                                     tt_ * 128:
                                                     hc * 1024 + tt_ * 128
                                                     + 128],
                                             identity=ident[:]),
                                         waits=pw if j == 0 else
                                         ([(f"ldx{w}", ldx)] if j == 1 and
                                          j0 == 0 else []))
                        P.add_inc(rec, "pe", 1)
                        pev = P.cnt["pe"]
                        rec = P.emit("vector", lambda p, j0_=j0, hf=half,
                                     w_=w: p.tensor_tensor(
                                         out=xt[w_][:, j0_ * 128:
                                                    (j0_ + 4) * 128]
                                         .rearrange("p (a c) -> p a c",
                                                    c=128),
                                         in0=ps_b[:, hf * 1024:
                                                  hf * 1024 + 512]
                                         .rearrange("p (a c) -> p a c",
                                                    c=128),
                                         in1=xt[w_][:, j0_ * 128:
                                                    (j0_ + 4) * 128]
                                         .rearrange("p (a c) -> p a c",
                                                    c=128),
                                         op=ALU.add),
                                     waits=[("pe", pev)])
                        P.add_inc(rec, "dv", 1)
                        guard_set(("tp", half), "dv")
                    sty = P.dma("gpsimd", f"sgx{w}",
                                y_d[b, t0 + tt * 128:t0 + (tt + 1) * 128, :],
                                xt[w][:], waits=[("dv", P.cnt["dv"])])
                    xld[("fxt", w)] = [(f"sgx{w}", sty)]
                wld["fin_dv"] = P.cnt["dv"]
                wld["fin_pe"] = P.cnt["pe"]

        def p1tap(b):
            if debug_outputs and b == 0:
                P.dma("gpsimd", "sto", t_hTf, hT[:, :],
                      waits=[("dv", P.cnt["dv"])])
                P.dma("gpsimd", "sto", t_s1, dbg4[:, 0:16])
                P.dma("gpsimd", "sto", t_ss, dbg4[:, 16:32])
                P.dma("gpsimd", "sto", t_var, dbg4[:, 32:48])
                P.dma("gpsimd", "sto", t_rstd, dbg4[:, 48:64])
        phases = [("P0", None)]
        for b in range(BPC):
            phases += [(f"P1b{b}",
                        lambda b=b: (p1(b), p1tap(b), P.barrier())),
                       (f"ATTNb{b}", lambda b=b: (attn(b), P.barrier())),
                       (f"RESb{b}", lambda b=b: (res(b), P.barrier()))]
        def wflush():
            P.barrier()
        phases.append(("WFLUSH", wflush))
        phases.append(("FFN", ffn))
        for name, fn in phases:
            if fn is not None:
                fn()
            if stop_after is not None and name == stop_after:
                break
        P.emit("gpsimd", None,
               waits=[(s_, P.cnt[s_]) for s_ in
                      ("sta0", "sta1", "sto", "sgx0", "sgx1", "sth")
                      if P.cnt[s_] > 0])

        with nc.Block() as block:
            def runner(name):
                def _run(proxy):
                    for build_fn, waits, incs in P.ops[name]:
                        for s, v in waits:
                            if v > 0:
                                proxy.wait_ge(sems[s], v)
                        if build_fn is not None:
                            ins = build_fn(proxy)
                            for s, n in incs:
                                ins.then_inc(sems[s], n)
                return _run
            block.sync(runner("sync"))
            block.vector(runner("vector"))
            block.scalar(runner("scalar"))
            block.tensor(runner("tensor"))
            block.gpsimd(runner("gpsimd"))

    return nc


def kernel(L0_attn_x, L0_attn_Wq, L0_attn_bq, L0_attn_Wk, L0_attn_bk,
           L0_attn_Wv, L0_attn_bv, L0_ffn_lut, L0_ffn_gate_idx,
           L0_ffn_up_idx, L0_ffn_down_idx):
    x = np.ascontiguousarray(np.asarray(L0_attn_x, dtype=np.float32))
    wq = np.ascontiguousarray(np.asarray(L0_attn_Wq, dtype=np.float32))
    bq = np.ascontiguousarray(np.asarray(L0_attn_bq, dtype=np.float32))
    wk = np.ascontiguousarray(np.asarray(L0_attn_Wk, dtype=np.float32))
    bk = np.ascontiguousarray(np.asarray(L0_attn_bk, dtype=np.float32))
    wv = np.ascontiguousarray(np.asarray(L0_attn_Wv, dtype=np.float32))
    bv = np.ascontiguousarray(np.asarray(L0_attn_bv, dtype=np.float32))
    lut = np.asarray(L0_ffn_lut, dtype=np.float32)
    gidxT = np.ascontiguousarray(np.asarray(L0_ffn_gate_idx).T.astype(np.int8))
    uidxT = np.ascontiguousarray(np.asarray(L0_ffn_up_idx).T.astype(np.int8))
    didxT = np.ascontiguousarray(np.asarray(L0_ffn_down_idx).T.astype(np.int8))

    ident = np.eye(128, dtype=ml_dtypes.bfloat16)
    qi = np.arange(128)
    maskq = np.where(qi[None, :] <= qi[:, None], 0.0,
                     NEG).astype(ml_dtypes.bfloat16)
    maskt = np.ascontiguousarray(maskq.T)

    lutvals = [float(v) for v in lut]
    nc = bass.Bass("TRN2", target_bir_lowering=False,
                   detect_race_conditions=False)
    build(nc, lutvals)

    in_maps = []
    for c in range(NCORES):
        in_maps.append({
            "x": x[c * BPC:(c + 1) * BPC],
            "wq": wq, "bq": bq, "wk": wk, "bk": bk, "wv": wv, "bv": bv,
            "gidxT": gidxT, "uidxT": uidxT, "didxT": didxT,
            "ident": ident, "maskq": maskq, "maskt": maskt,
        })
    res = run_bass_kernel_spmd(nc, in_maps, core_ids=list(range(NCORES)))
    out = np.concatenate([res.results[c]["y"] for c in range(NCORES)],
                         axis=0)
    return out.astype(np.float32)


# version 15
# speedup vs baseline: 1.6788x; 1.6788x over previous
import math
from contextlib import ExitStack

import numpy as np
import ml_dtypes

import concourse.bass as bass
import concourse.mybir as mybir
from concourse.bass_utils import run_bass_kernel_spmd
from concourse import dve_ops as _dve_ops
from concourse.dve_op import DveOp as _DveOp
from concourse.dve_spec import (Spec as _Spec, Src0 as _S0, Src1 as _S1,
                                C0 as _C0, C1 as _C1, C2 as _C2, One as _One,
                                Zero as _Zero, select as _sel, eq as _eq,
                                lower as _lower)


def _lut2_ref(in0, in1, s0, s1, imm2):
    r = np.where(np.asarray(in0) == imm2, s0,
                 np.where(np.asarray(in0) == imm2 + 1.0, s1, in1))
    return r.astype(np.asarray(in1).dtype)


def _register_lut2():
    if any(o.name == "LUT2_SEL" for o in _dve_ops.OPS):
        return next(o for o in _dve_ops.OPS if o.name == "LUT2_SEL")
    spec = _Spec(body=(_S1 + _eq(_S0, _C2) * _C0) +
                 _eq(_S0 - _C2, _One) * _C1)
    shas = {}
    for ver in ("v3",):
        shas[ver] = _lower(spec, ver=ver, perf_en=False,
                           name="LUT2_SEL").sha(ver)
    op = _DveOp("LUT2_SEL", spec, subdim=False, uops_sha=shas,
                reference=_lut2_ref)
    row = max(_dve_ops._SUB_OPCODE_FOR_NAME.values()) + 1
    assert row < 0x20, row
    _dve_ops.OPS.append(op)
    _dve_ops._SUB_OPCODE_FOR_NAME["LUT2_SEL"] = row
    _dve_ops.CUSTOM_DVE_SPECS["LUT2_SEL"] = spec
    _dve_ops.CUSTOM_DVE_REFS["LUT2_SEL"] = _lut2_ref
    return op


_LUT2 = _register_lut2()

F32 = mybir.dt.float32
BF16 = mybir.dt.bfloat16
I32 = mybir.dt.int32
I8 = mybir.dt.int8
AF = mybir.ActivationFunctionType
ALU = mybir.AluOpType
AXX = mybir.AxisListType.X

S = 2048
E = 2048
H = 16
G = 4
D = 128
PER = H // G
INTER = 5504
OC = INTER // 128
EC = E // 128
NT = S // 128
BPC = 2
NCORES = 8
CINV = 1.0 / math.sqrt(D)
LNEPS = 1e-5
NEG = -1.0e30

ENGS = ("sync", "vector", "scalar", "tensor", "gpsimd")
QSEM = {"sync": "ld", "gpsimd": "st"}


class Prog:

    def __init__(self, nc, sems):
        self.nc = nc
        self.sems = sems
        self.cnt = {k: 0 for k in sems}
        self.ops = {e: [] for e in ENGS}
        self.last = {e: None for e in ENGS}
        self.dirty = {e: False for e in ENGS}

    def emit(self, eng, build=None, waits=(), incs=()):
        for s, n in incs:
            self.cnt[s] += n
        rec = [build, list(waits), list(incs)]
        self.ops[eng].append(rec)
        if build is not None:
            self.last[eng] = rec
            self.dirty[eng] = True
        return rec

    def dma(self, q, sem, out, in_, waits=()):
        self.emit(q, lambda p, o=out, i=in_: p.dma_start(out=o, in_=i),
                  waits, [(sem, 16)])
        return self.cnt[sem]

    def add_inc(self, rec, sem, n=1):
        for k, (s0, n0) in enumerate(rec[2]):
            if s0 == sem:
                rec[2][k] = (s0, n0 + n)
                self.cnt[sem] += n
                return
        assert not rec[2], f"instruction already updates {rec[2][0][0]}"
        rec[2].append((sem, n))
        self.cnt[sem] += n

    def barrier(self):
        for e, s in (("vector", "dv"), ("scalar", "ac"), ("tensor", "pe")):
            if self.dirty[e]:
                if not self.last[e][2]:
                    self.add_inc(self.last[e], s, 1)
        tgt = {s: self.cnt[s] for s in self.sems}
        own = {"vector": "dv", "scalar": "ac", "tensor": "pe"}
        for e in ENGS:
            waits = [(s, v) for s, v in tgt.items()
                     if s != own.get(e) and v > 0]
            self.emit(e, None, waits=waits)
        for e in ENGS:
            self.dirty[e] = False


def build(nc, lutvals, taps=(), stop_after=None, debug_outputs=False):
    dt = nc.dram_tensor
    DBG = "ExternalOutput" if debug_outputs else "Internal"
    x_d = dt("x", [BPC, S, E], F32, kind="ExternalInput").ap()
    wq_d = dt("wq", [H, E, D], F32, kind="ExternalInput").ap()
    bq_d = dt("bq", [H, D], F32, kind="ExternalInput").ap()
    wk_d = dt("wk", [G, E, D], F32, kind="ExternalInput").ap()
    bk_d = dt("bk", [G, D], F32, kind="ExternalInput").ap()
    wv_d = dt("wv", [G, E, D], F32, kind="ExternalInput").ap()
    bv_d = dt("bv", [G, D], F32, kind="ExternalInput").ap()
    gidxT_d = dt("gidxT", [E, INTER], I8, kind="ExternalInput").ap()
    uidxT_d = dt("uidxT", [E, INTER], I8, kind="ExternalInput").ap()
    didxT_d = dt("didxT", [INTER, E], I8, kind="ExternalInput").ap()
    ident_d = dt("ident", [128, 128], BF16, kind="ExternalInput").ap()
    maskq_d = dt("maskq", [128, 128], BF16, kind="ExternalInput").ap()
    maskt_d = dt("maskt", [128, 128], BF16, kind="ExternalInput").ap()
    y_d = dt("y", [BPC, S, E], F32, kind="ExternalOutput").ap()
    wg_s = dt("wg_s", [EC, 128, INTER], BF16, kind=DBG).ap()
    wu_s = dt("wu_s", [EC, 128, INTER], BF16, kind=DBG).ap()
    wd_s = dt("wd_s", [OC, 128, E], BF16, kind=DBG).ap()
    x2_s = dt("x2_s", [BPC, S, E], F32, kind=DBG).ap()
    h2_s = dt("h2_s", [BPC, EC, 128, S], BF16, kind=DBG).ap()
    o_s = dt("o_s", [BPC, H, 128, S], BF16, kind=DBG).ap()
    mr_s = dt("mr_s", [16, 128], BF16, kind="Internal").ap()
    if debug_outputs:
        t_qT = dt("t_qT", [128, S], BF16, kind="ExternalOutput").ap()
        t_kT = dt("t_kT", [128, S], BF16, kind="ExternalOutput").ap()
        t_v = dt("t_v", [128, S], BF16, kind="ExternalOutput").ap()
        t_mrow = dt("t_mrow", [1, S], BF16, kind="ExternalOutput").ap()
        t_mcol = dt("t_mcol", [128, 16], BF16, kind="ExternalOutput").ap()
        t_l = dt("t_l", [128, 16], F32, kind="ExternalOutput").ap()
        t_pT = dt("t_pT", [128, S], BF16, kind="ExternalOutput").ap()
        t_hT = dt("t_hT", [128, S], BF16, kind="ExternalOutput").ap()
        t_hTf = dt("t_hTf", [128, EC * S], BF16,
                   kind="ExternalOutput").ap()
        t_s1 = dt("t_s1", [128, 16], F32, kind="ExternalOutput").ap()
        t_ss = dt("t_ss", [128, 16], F32, kind="ExternalOutput").ap()
        t_var = dt("t_var", [128, 16], F32, kind="ExternalOutput").ap()
        t_rstd = dt("t_rstd", [128, 16], F32, kind="ExternalOutput").ap()

    tap_d = {}
    for t, shape, dtt in taps:
        tap_d[t] = dt("tap_" + t, shape, dtt, kind="ExternalOutput").ap()

    with ExitStack() as ctx:
        en = ctx.enter_context

        SBASE = 16896

        def sb(name, cols, dtyp, off):
            return nc.alloc_sbuf_tensor_at(name, [128, cols], dtyp,
                                           offset=SBASE + off).ap()

        maskq = sb("maskq_sb", 128, BF16, 0)
        maskt = sb("maskt_sb", 128, BF16, 512)
        ident = sb("ident_sb", 128, BF16, 1024)
        bq_all = sb("bq_all", H, F32, 1280)
        bk_all = sb("bk_all", G, F32, 1344)
        bv_all = sb("bv_all", G, F32, 1376)
        stats = sb("stats", 16, F32, 1408)
        mcolb = sb("mcolb", 18, BF16, 1472)
        l_all = sb("l_all", 16, F32, 9088)
        lrec = sb("lrec", 256, F32, 1568)
        b1a = sb("b1a", 16, F32, 7232)
        c_cinv = sb("c_cinv", 1, F32, 7296)
        dbg4 = sb("dbg4", 64, F32, 7424)
        onesb = nc.alloc_sbuf_tensor_at("onesb", [1, 128], BF16,
                                        offset=SBASE + 2592).ap()
        mrowT = nc.alloc_sbuf_tensor_at("mrowT", [16, 128], BF16,
                                        offset=SBASE + 2848).ap()
        mrow = nc.alloc_sbuf_tensor_at("mrow", [1, S], BF16,
                                       offset=SBASE + 3104).ap()

        B1 = 12 * 1024
        B2 = B1 + 65536
        WK = B2 + 65536
        WKE = WK + 61440
        assert SBASE + WKE <= 224 * 1024

        hT = sb("hT", EC * S, BF16, B1)
        pT = sb("pT", NT * S, BF16, B2)
        hh = sb("hh", OC * 1024, BF16, B1)
        h2Tf = sb("h2Tf", EC * 1024, BF16, B1 + 88064)
        dna = sb("dna", EC * 1024, BF16, B1 + 88064)


        wstg2 = [sb(f"wstg2{j}", 512, F32, WK + j * 2048) for j in (0, 1)]
        wq_b = sb("wq_b", E, BF16, WK + 4096)
        wk_b = sb("wk_b", E, BF16, WK + 8192)
        wv_b = sb("wv_b", E, BF16, WK + 12288)
        qT = sb("qT", S, BF16, WK + 16384)
        kT = sb("kT", S, BF16, WK + 20480)
        vt_b = sb("vt_b", S, BF16, WK + 24576)
        v_sb = sb("v_sb", S, BF16, WK + 28672)
        expsc = sb("expsc", 1024, BF16, WK + 32768)
        oT_sb = sb("oT_sb", S, BF16, WK + 34816)
        WSLICE = 688
        idxh = [sb(f"idxh{j}", WSLICE, I8, WK + 38912 + j * 704)
                for j in (0, 1)] + \
               [sb(f"idxh{j}", WSLICE, I8, 9152 + (j - 2) * 704)
                for j in (2, 3)]
        idxbf = sb("idxbf", WSLICE, BF16, WK + 40320)
        accw = [sb(f"accw{j}", WSLICE, BF16, WK + 41728 + j * 1408)
                for j in (0, 1)]
        termw = sb("termw", WSLICE, BF16, 7680)
        xt = [sb(f"xt{j}", E, F32, WK + 45056 + j * 8192) for j in (0, 1)]
        h2b = sb("h2b", E, BF16, WK + 20480)
        h2a = sb("h2a", E, BF16, WK + 24576)
        otin = sb("otin", E, BF16, WK + 28672)
        wgt = [sb(f"wgt{j}", EC * 128, BF16, WK + j * 4096) for j in (0, 1)]
        wut = [sb(f"wut{j}", EC * 128, BF16, WK + 8192 + j * 4096)
               for j in (0, 1)]
        wdt = [sb(f"wdt{j}", 2816, BF16, WK + 16384 + j * 5632)
               for j in (0, 1)]
        sil = sb("sil", 1536, BF16, WK + 27648)

        ps_f = en(nc.psum_tensor("ps_f", [128, 3072], F32)).ap()
        ps_b = en(nc.psum_tensor("ps_b", [128, 2048], BF16)).ap()
        RA = ps_f[:, 0:2048]

        sems = {}
        for s in ("dv", "ac", "pe",
                  "ld0", "ldi0", "ldi1", "ldi2", "ldi3", "ldx0", "ldx1", "ldo", "ldw0",
                  "ldm", "ldh", "ldg0", "ldg1", "ldu0", "ldu1", "ldw1",
                  "ldd0", "ldd1",
                  "sta0", "sta1", "sto", "sgx0", "sgx1", "sth"):
            sems[s] = en(nc.semaphore(s))
        P = Prog(nc, sems)

        guards = {}

        def guard_wait(key):
            return guards.pop(key, [])

        def guard_set(key, sem):
            guards.setdefault(key, []).append((sem, P.cnt[sem]))

        P.dma("sync", "ld0", maskq, maskq_d)
        P.dma("sync", "ld0", maskt, maskt_d)
        P.dma("sync", "ld0", ident, ident_d)
        for hh_ in range(H):
            P.dma("sync", "ld0", bq_all[:, hh_:hh_ + 1],
                  bq_d[hh_].rearrange("(d o) -> d o", o=1))
        for g in range(G):
            P.dma("sync", "ld0", bk_all[:, g:g + 1],
                  bk_d[g].rearrange("(d o) -> d o", o=1))
            P.dma("sync", "ld0", bv_all[:, g:g + 1],
                  bv_d[g].rearrange("(d o) -> d o", o=1))
        P.emit("vector", lambda p: p.memset(onesb[:], 1.0), [], [("dv", 1)])
        P.barrier()

        wsteps = []
        for hc in range(EC):
            for c0 in range(0, INTER, WSLICE):
                cw = min(WSLICE, INTER - c0)
                wsteps.append((gidxT_d[hc * 128:(hc + 1) * 128, c0:c0 + cw],
                               wg_s[hc][:, c0:c0 + cw], cw))
        for hc in range(EC):
            for c0 in range(0, INTER, WSLICE):
                cw = min(WSLICE, INTER - c0)
                wsteps.append((uidxT_d[hc * 128:(hc + 1) * 128, c0:c0 + cw],
                               wu_s[hc][:, c0:c0 + cw], cw))
        for oc in range(OC):
            for c0 in range(0, E, WSLICE):
                cw = min(WSLICE, E - c0)
                wsteps.append((didxT_d[oc * 128:(oc + 1) * 128, c0:c0 + cw],
                               wd_s[oc][:, c0:c0 + cw], cw))
        N_GU = 2 * (INTER + WSLICE - 1) // WSLICE * EC
        wS = {"t": 0, "dv": [0] * len(wsteps), "st": [0] * len(wsteps),
              "ld": [0] * len(wsteps), "carry": 0.0}

        def w_pace(limit, ration):
            wS["carry"] += ration
            n = int(wS["carry"])
            wS["carry"] -= n
            while n > 0 and wS["t"] < limit:
                w_step_one()
                n -= 1

        def w_step(n, limit=None):
            if limit is None:
                limit = len(wsteps)
            for _ in range(n):
                if wS["t"] >= limit:
                    return
                w_step_one()

        def w_step_one():
                t = wS["t"]
                wS["t"] = t + 1
                src, dst, w = wsteps[t]
                ib = idxh[t % 4][:, 0:w]
                ab = accw[t % 2][:, 0:w]
                waits = [("dv", wS["dv"][t - 4])] if t >= 4 else []
                wS["ld"][t] = P.dma("sync", f"ldi{t % 4}", ib, src,
                                    waits=waits)
                P.emit("vector",
                       lambda p, i=ib, w_=w: p.tensor_copy(
                           out=idxbf[:, 0:w_], in_=i),
                       [(f"ldi{t % 4}", wS["ld"][t])] +
                       ([(f"sta{t % 2}", wS["st"][t - 2])] if t >= 2 else []))
                P.emit("vector",
                       lambda p, a=ab, w_=w: p.tensor_scalar(
                           out=a, in0=idxbf[:, 0:w_], scalar1=0.0,
                           scalar2=lutvals[0], op0=ALU.is_equal,
                           op1=ALU.mult))
                for tv in range(1, 16):
                    P.emit("vector",
                           lambda p, w_=w, tv_=tv: p.tensor_scalar(
                               out=termw[:, 0:w_], in0=idxbf[:, 0:w_],
                               scalar1=float(tv_), scalar2=lutvals[tv_],
                               op0=ALU.is_equal, op1=ALU.mult))
                    rec = P.emit("vector",
                                 lambda p, a=ab, w_=w: p.tensor_tensor(
                                     out=a, in0=a, in1=termw[:, 0:w_],
                                     op=ALU.add))
                P.add_inc(rec, "dv", 1)
                wS["dv"][t] = P.cnt["dv"]
                wS["st"][t] = P.dma("gpsimd", f"sta{t % 2}", dst, ab,
                                    waits=[("dv", wS["dv"][t])])

        s1c = stats[:, 0:1]
        nmc = stats[:, 1:2]
        ssc = stats[:, 2:3]
        varc = stats[:, 3:4]
        rstd = stats[:, 4:5]
        sqsc = stats[:, 5:6]
        mf32 = stats[:, 6:7]
        b1c = stats[:, 7:8]

        def ln_tile(xtile, out_bf, sq_out, dbgi=None):
            P.ops["vector"].append([None, [("pe", P.cnt["pe"])], []])
            rec = P.emit("vector", lambda p: p.reduce_sum(out=s1c, in_=xtile,
                                                          axis=AXX))
            P.add_inc(rec, "dv", 1)
            rec = P.emit("vector", lambda p: p.tensor_scalar(
                out=nmc, in0=s1c, scalar1=-1.0 / E, scalar2=None,
                op0=ALU.mult), waits=[("dv", P.cnt["dv"])])
            P.add_inc(rec, "dv", 1)
            dvv = P.cnt["dv"]
            rec = P.emit("scalar", lambda p: p.activation(
                out=sq_out, in_=xtile, func=AF.Square, bias=0.0, scale=1.0,
                accum_out=ssc), waits=[("dv", dvv)])
            P.add_inc(rec, "ac", 1)
            acv = P.cnt["ac"]
            P.emit("vector", lambda p: p.tensor_tensor(
                out=sqsc, in0=nmc, in1=nmc, op=ALU.mult),
                waits=[("ac", acv)])
            rec = P.emit("vector", lambda p: p.tensor_scalar(
                out=varc, in0=ssc, scalar1=1.0 / E, scalar2=LNEPS,
                op0=ALU.mult, op1=ALU.add))
            P.add_inc(rec, "dv", 1)
            rec = P.emit("vector", lambda p: p.tensor_tensor(
                out=varc, in0=varc, in1=sqsc, op=ALU.subtract),
                waits=[("dv", P.cnt["dv"])])
            P.add_inc(rec, "dv", 1)
            dvv = P.cnt["dv"]
            rec = P.emit("scalar", lambda p: p.activation(
                out=sqsc, in_=varc, func=AF.Sqrt, bias=0.0, scale=1.0),
                waits=[("dv", dvv)])
            P.add_inc(rec, "ac", 1)
            acv = P.cnt["ac"]
            rec = P.emit("vector", lambda p: p.reciprocal(out=rstd,
                                                          in_=sqsc),
                         waits=[("ac", acv)])
            P.add_inc(rec, "dv", 1)
            P.ops["vector"].append([None, [("dv", P.cnt["dv"])], []])
            if dbgi is not None:
                P.emit("vector", lambda p, i_=dbgi: p.tensor_copy(
                    out=dbg4[:, i_:i_ + 1], in_=s1c))
                P.emit("vector", lambda p, i_=dbgi: p.tensor_copy(
                    out=dbg4[:, 16 + i_:16 + i_ + 1], in_=ssc))
                P.emit("vector", lambda p, i_=dbgi: p.tensor_copy(
                    out=dbg4[:, 32 + i_:32 + i_ + 1], in_=varc))
                P.emit("vector", lambda p, i_=dbgi: p.tensor_copy(
                    out=dbg4[:, 48 + i_:48 + i_ + 1], in_=rstd))
            rec = P.emit("vector", lambda p: p.tensor_scalar(
                out=out_bf, in0=xtile, scalar1=nmc, scalar2=rstd,
                op0=ALU.add, op1=ALU.mult))
            P.add_inc(rec, "dv", 1)
            return P.cnt["dv"]

        def transpose_groups(src_bf, dst_bf16, nblk, src_off, dst_view,
                             pe_waits=()):
            for j0 in range(0, nblk, 4):
                nb = min(4, nblk - j0)
                half = (j0 // 4) % 2
                pw = list(pe_waits) + guard_wait(("tp", half))
                for j in range(nb):
                    rec = P.emit("tensor", lambda p, a=src_off + (j0 + j) * 128,
                                 hf=half, jj=j: p.transpose(
                                     out=ps_b[:, hf * 1024 + jj * 128:
                                              hf * 1024 + (jj + 1) * 128],
                                     in_=src_bf[:, a:a + 128],
                                     identity=ident[:]),
                                 waits=pw if j == 0 else [])
                P.add_inc(rec, "pe", 1)
                pev = P.cnt["pe"]
                rec = P.emit("vector", lambda p, hf=half, n=nb, j0_=j0:
                             p.tensor_copy(
                                 out=dst_view(j0_, n),
                                 in_=ps_b[:, hf * 1024:hf * 1024 + n * 128]
                                 .rearrange("p (a c) -> p a c", c=128)),
                             waits=[("pe", pev)])
                P.add_inc(rec, "dv", 1)
                guard_set(("tp", half), "dv")

        xld = {}

        def load_x_tile(b, i, which, src=x_d):
            waits = []
            key = ("xt", which)
            if key in xld:
                waits = xld[key]
            v = P.dma("sync", f"ldx{which}", xt[which][:],
                      src[b, i * 128:(i + 1) * 128, :], waits=waits)
            xld[key + ("ld",)] = v
            return v

        def x_tile_consumed(which, marks):
            xld[("xt", which)] = marks

        def p1(b):
            for i in range(NT):
                w = i % 2
                ldv = load_x_tile(b, i, w)
                P.ops["vector"].append([None, [(f"ldx{w}", ldv)], []])
                dvv = ln_tile(xt[w][:], h2b[:], h2b[:],
                              dbgi=(i if debug_outputs and b == 0 else None))
                def dv_hT(j0, n, i=i):
                    return hT[:, :].rearrange("p (ec t) -> p ec t", t=S)[
                        :, j0:j0 + n, i * 128:(i + 1) * 128]
                import os as _os
                if not _os.environ.get("K_NO_TRANS"):
                    transpose_groups(h2b, None, EC, 0, dv_hT,
                                     pe_waits=[("dv", dvv)])
                x_tile_consumed(w, [("dv", dvv), ("ac", P.cnt["ac"])])
                w_pace(N_GU, 0.22)

        slot_ctr = [0]

        def next_slot():
            sl = slot_ctr[0] % 3
            slot_ctr[0] += 1
            return sl

        def project(w_bf, bias_ap, out_sb, tag, scale=None):
            for pc in range(4):
                sl = next_slot()
                base = sl * 1024
                pw = guard_wait(("sl", sl))
                for ec in range(EC):
                    rec = P.emit("tensor", lambda p, e=ec, pc_=pc, b_=base:
                                 p.matmul(
                                     out=ps_f[:, b_:b_ + 512],
                                     lhsT=w_bf[:, e * 128:(e + 1) * 128],
                                     rhs=hT[:, e * S + pc_ * 512:
                                            e * S + pc_ * 512 + 512],
                                     start=(e == 0), stop=(e == EC - 1)),
                                 waits=pw if ec == 0 else [])
                P.add_inc(rec, "pe", 1)
                pev = P.cnt["pe"]
                if scale is None:
                    rec = P.emit("scalar", lambda p, pc_=pc, b_=base:
                                 p.activation(
                                     out=out_sb[:, pc_ * 512:
                                                pc_ * 512 + 512],
                                     in_=ps_f[:, b_:b_ + 512],
                                     func=AF.Identity, bias=bias_ap,
                                     scale=1.0),
                                 waits=[("pe", pev)])
                    P.add_inc(rec, "ac", 1)
                    guard_set(("sl", sl), "ac")
                else:
                    rec = P.emit("vector", lambda p, pc_=pc, b_=base,
                                 sc_=scale: p.tensor_scalar(
                                     out=out_sb[:, pc_ * 512:
                                                pc_ * 512 + 512],
                                     in0=ps_f[:, b_:b_ + 512],
                                     scalar1=bias_ap, scalar2=sc_,
                                     op0=ALU.add, op1=ALU.mult),
                                 waits=[("pe", pev)])
                    P.add_inc(rec, "dv", 1)
                    guard_set(("sl", sl), "dv")
            return P.cnt["ac"], P.cnt["pe"]

        wcast = {"j": 0}

        def load_cast_w(src_ap, dst_bf, tag):
            src = src_ap.rearrange("(ec p) d -> p ec d", p=128)
            for j in range(4):
                wb = wcast["j"] % 2
                wcast["j"] += 1
                ldv = P.dma("sync", f"ldw{wb}",
                            wstg2[wb][:, :].rearrange("p (e d) -> p e d",
                                                      d=128),
                            src[:, 4 * j:4 * j + 4, :],
                            waits=[("dv", wcast.get(("c", wb), 0))])
                rec = P.emit("vector",
                             lambda p, d=dst_bf, j_=j, w_=wb: p.tensor_copy(
                                 out=d[:, j_ * 512:(j_ + 1) * 512],
                                 in_=wstg2[w_][:]),
                             waits=[(f"ldw{wb}", ldv),
                                    ("pe", wcast.get(tag, 0) if j == 0
                                     else 0)])
                P.add_inc(rec, "dv", 1)
                wcast[("c", wb)] = P.cnt["dv"]
            return P.cnt["dv"]

        lp0 = stats[:, 8:9]
        lp1 = stats[:, 9:10]
        nmh = stats[:, 10:11]

        def attn(b):
            for g in range(G):
                dvv = load_cast_w(wk_d[g], wk_b, "wk")
                P.ops["tensor"].append([None, [("dv", dvv)], []])
                kdone, pemark = project(wk_b, bk_all[:, g:g + 1], kT, "k")
                wcast["wk"] = pemark
                dvv = load_cast_w(wv_d[g], wv_b, "wv")
                P.ops["tensor"].append([None, [("dv", dvv)], []])
                vdone, pemark = project(wv_b, bv_all[:, g:g + 1], vt_b, "v")
                wcast["wv"] = pemark

                def dv_v(j0, n):
                    return v_sb[:, j0 * 128:(j0 + n) * 128].rearrange(
                        "p (a c) -> p a c", c=128)
                transpose_groups(vt_b, None, NT, 0, dv_v,
                                 pe_waits=[("ac", vdone)])
                for h in range(g * PER, (g + 1) * PER):
                    dvv = load_cast_w(wq_d[h], wq_b, "wq")
                    P.ops["tensor"].append([None, [("dv", dvv)], []])
                    _, pemark = project(wq_b, bq_all[:, h:h + 1], qT,
                                        "q", scale=CINV)
                    wcast["wq"] = pemark
                    for i in range(NT):
                        ext = 128 * (i + 1)
                        nh = (ext + 1023) // 1024
                        tile_slots = []
                        for hf in range(nh):
                            c0 = 1024 * hf
                            cwH = min(1024, ext - c0)
                            sl = next_slot()
                            base = sl * 1024
                            pw = guard_wait(("sl", sl))
                            doff = ext - 128 - c0
                            for j in range((cwH + 511) // 512):
                                cj = j * 512
                                cw = min(512, cwH - cj)
                                hasdiag = (hf == nh - 1 and
                                           cj <= doff < cj + cw)
                                rec = P.emit(
                                    "tensor",
                                    lambda p, i_=i, b_=base + cj,
                                    s_=c0 + cj, c_=cw, sp=(not hasdiag):
                                    p.matmul(
                                        out=ps_f[:, b_:b_ + c_],
                                        lhsT=qT[:, i_ * 128:(i_ + 1) * 128],
                                        rhs=kT[:, s_:s_ + c_],
                                        start=True, stop=sp),
                                    waits=pw if j == 0 else [])
                                if hasdiag:
                                    rec = P.emit(
                                        "tensor",
                                        lambda p, b_=base + doff: p.matmul(
                                            out=ps_f[:, b_:b_ + 128],
                                            lhsT=ident[:], rhs=maskq[:],
                                            start=False, stop=True))
                            P.add_inc(rec, "pe", 1)
                            pev = P.cnt["pe"]
                            tgt = (mcolb[:, i:i + 1] if hf == 0
                                   else mcolb[:, 16:17])
                            rec = P.emit(
                                "vector",
                                lambda p, b_=base, c_=cwH, t_=tgt:
                                p.reduce_max(out=t_, in_=ps_f[:, b_:b_ + c_],
                                             axis=AXX, negate=True),
                                waits=[("pe", pev)])
                            P.add_inc(rec, "dv", 1)
                            if hf > 0:
                                rec = P.emit(
                                    "vector",
                                    lambda p, i_=i: p.tensor_tensor(
                                        out=mcolb[:, i_:i_ + 1],
                                        in0=mcolb[:, i_:i_ + 1],
                                        in1=mcolb[:, 16:17], op=ALU.min),
                                    waits=[("dv", P.cnt["dv"])])
                                P.add_inc(rec, "dv", 1)
                            tile_slots.append((sl, c0, cwH))
                        rec = P.emit("vector", lambda p, i_=i:
                                     p.tensor_copy(
                                         out=b1a[:, i_:i_ + 1],
                                         in_=mcolb[:, i_:i_ + 1]),
                                     waits=[("dv", P.cnt["dv"])])
                        P.add_inc(rec, "dv", 1)
                        b1v = P.cnt["dv"]
                        for hf, (sl, c0, cwH) in enumerate(tile_slots):
                            lp = (l_all[:, i:i + 1] if nh == 1
                                  else (lp0 if hf == 0 else lp1))
                            rec = P.emit(
                                "scalar",
                                lambda p, b_=sl * 1024, c_=cwH, i_=i,
                                lp_=lp: p.activation(
                                    out=expsc[:, 0:c_],
                                    in_=ps_f[:, b_:b_ + c_], func=AF.Exp,
                                    bias=b1a[:, i_:i_ + 1], scale=1.0,
                                    accum_out=lp_),
                                waits=[("dv", b1v)])
                            P.add_inc(rec, "ac", 1)
                            guard_set(("sl", sl), "ac")
                        acv = P.cnt["ac"]
                        if nh > 1:
                            rec = P.emit("vector", lambda p, i_=i:
                                         p.tensor_tensor(
                                             out=l_all[:, i_:i_ + 1],
                                             in0=lp0, in1=lp1, op=ALU.add),
                                         waits=[("ac", acv)])
                            P.add_inc(rec, "dv", 1)
                        w_pace(N_GU, 0.22)
                    rec = P.emit("vector", lambda p, h_=h: p.reciprocal(
                        out=lrec[:, h_ * 16:h_ * 16 + 16], in_=l_all[:]),
                        waits=[("dv", P.cnt["dv"])])
                    P.add_inc(rec, "dv", 1)
                    pw = guard_wait(("tp", 0)) + [("dv", P.cnt["dv"])]
                    rec = P.emit("tensor", lambda p: p.transpose(
                        out=ps_b[0:16, 0:128], in_=mcolb[:, 0:16],
                        identity=ident[:]), waits=pw)
                    P.add_inc(rec, "pe", 1)
                    rec = P.emit("vector", lambda p: p.tensor_copy(
                        out=mrowT[:], in_=ps_b[0:16, 0:128]),
                        waits=[("pe", P.cnt["pe"])])
                    P.add_inc(rec, "dv", 1)
                    guard_set(("tp", 0), "dv")
                    m1 = P.dma("sync", "ldm", mr_s, mrowT,
                               waits=[("dv", P.cnt["dv"])])
                    mld = P.dma("sync", "ldm", mrow,
                                mr_s.rearrange("i p -> (i p)").rearrange(
                                    "(o q) -> o q", o=1),
                                waits=[("ldm", m1)])
                    if debug_outputs and b == 0 and h == 0:
                        P.dma("gpsimd", "sto", t_qT, qT,
                              waits=[("ac", P.cnt["ac"])])
                        P.dma("gpsimd", "sto", t_kT, kT)
                        P.dma("gpsimd", "sto", t_v, v_sb)
                        P.dma("gpsimd", "sto", t_mrow, mrow,
                              waits=[("ldm", mld)])
                        P.dma("gpsimd", "sto", t_mcol, mcolb,
                              waits=[("dv", P.cnt["dv"])])
                        P.dma("gpsimd", "sto", t_l, l_all)
                        P.dma("gpsimd", "sto", t_hT, hT[:, 0:S])
                    for kc in range(NT):
                        w = S - kc * 128
                        for hf in range((w + 1023) // 1024):
                            c0 = kc * 128 + 1024 * hf
                            cwH = min(1024, S - c0)
                            sl = next_slot()
                            base = sl * 1024
                            pw = guard_wait(("sl", sl)) + [("ldm", mld)]
                            for j in range((cwH + 511) // 512):
                                q0 = c0 + j * 512
                                cw = min(512, S - q0)
                                rec = P.emit(
                                    "tensor",
                                    lambda p, kc_=kc, q0_=q0, cw_=cw,
                                    b_=base + j * 512: p.matmul(
                                        out=ps_f[:, b_:b_ + cw_],
                                        lhsT=kT[:, kc_ * 128:
                                                (kc_ + 1) * 128],
                                        rhs=qT[:, q0_:q0_ + cw_],
                                        start=True, stop=False),
                                    waits=pw if j == 0 else [])
                                rec = P.emit(
                                    "tensor",
                                    lambda p, q0_=q0, cw_=cw,
                                    b_=base + j * 512,
                                    sp=not (hf == 0 and j == 0): p.matmul(
                                        out=ps_f[:, b_:b_ + cw_],
                                        lhsT=onesb[:],
                                        rhs=mrow[0:1, q0_:q0_ + cw_],
                                        start=False, stop=sp))
                                if hf == 0 and j == 0:
                                    rec = P.emit(
                                        "tensor",
                                        lambda p, b_=base: p.matmul(
                                            out=ps_f[:, b_:b_ + 128],
                                            lhsT=ident[:], rhs=maskt[:],
                                            start=False, stop=True))
                            P.add_inc(rec, "pe", 1)
                            pev = P.cnt["pe"]
                            ew = [("pe", pev)]
                            rec = P.emit(
                                "scalar",
                                lambda p, kc_=kc, c0_=c0, c_=cwH, b_=base:
                                p.activation(
                                    out=pT[:, kc_ * S + c0_:
                                           kc_ * S + c0_ + c_],
                                    in_=ps_f[:, b_:b_ + c_], func=AF.Exp,
                                    bias=0.0, scale=1.0),
                                waits=ew)
                            P.add_inc(rec, "ac", 1)
                            guard_set(("sl", sl), "ac")
                            w_pace(N_GU, 0.22)
                    if debug_outputs and b == 0 and h == 0:
                        P.dma("gpsimd", "sto", t_pT, pT[:, 0:S],
                              waits=[("ac", P.cnt["ac"])])
                    pexp = P.cnt["ac"]
                    for qc in range(4):
                        sl = next_slot()
                        base = sl * 1024
                        pw = guard_wait(("sl", sl)) + [("ac", pexp)]
                        last_kc = min(NT - 1, 4 * qc + 3)
                        for kc in range(last_kc + 1):
                            qstart = max(qc * 512, kc * 128)
                            cw = qc * 512 + 512 - qstart
                            off = qstart - qc * 512
                            rec = P.emit(
                                "tensor",
                                lambda p, kc_=kc, qs=qstart, c=cw, o=off,
                                b_=base, s0=(kc == 0),
                                s1=(kc == last_kc): p.matmul(
                                    out=ps_f[:, b_ + o:b_ + o + c],
                                    lhsT=v_sb[:, kc_ * 128:(kc_ + 1) * 128],
                                    rhs=pT[:, kc_ * S + qs:
                                           kc_ * S + qs + c],
                                    start=s0, stop=s1),
                                waits=pw if kc == 0 else [])
                        P.add_inc(rec, "pe", 1)
                        pev = P.cnt["pe"]
                        rec = P.emit("scalar", lambda p, qc_=qc, b_=base:
                                     p.activation(
                                         out=oT_sb[:, qc_ * 512:
                                                   qc_ * 512 + 512],
                                         in_=ps_f[:, b_:b_ + 512],
                                         func=AF.Copy, bias=0.0, scale=1.0),
                                     waits=[("pe", pev)])
                        P.add_inc(rec, "ac", 1)
                        guard_set(("sl", sl), "ac")
                    stv = P.dma("gpsimd", "sto", o_s[b, h], oT_sb,
                                waits=[("ac", P.cnt["ac"])])
                    P.ops["scalar"].append([None, [("sto", stv)], []])

        def res(b):
            for i in range(NT):
                w = i % 2
                ldx = load_x_tile(b, i, w)
                ldo = P.dma("sync", "ldo", otin[:, :].rearrange(
                    "p (a c) -> p a c", c=128),
                    o_s[b, :, :, i * 128:(i + 1) * 128].rearrange(
                        "a p c -> p a c"),
                    waits=[("dv", xld.get(("otin",), 0))])
                for j0 in range(0, H, 4):
                    half = (j0 // 4) % 2
                    pw = guard_wait(("tp", half)) + \
                        ([("ldo", ldo)] if j0 == 0 else [])
                    for j in range(4):
                        rec = P.emit("tensor",
                                     lambda p, a=(j0 + j) * 128, hf=half,
                                     jj=j: p.transpose(
                                         out=ps_b[:, hf * 1024 + jj * 128:
                                                  hf * 1024 + (jj + 1) * 128],
                                         in_=otin[:, a:a + 128],
                                         identity=ident[:]),
                                     waits=pw if j == 0 else
                                     ([(f"ldx{w}", ldx)] if j == 1 and
                                      j0 == 0 else []))
                    P.add_inc(rec, "pe", 1)
                    pev = P.cnt["pe"]
                    for j in range(4):
                        hh_ = j0 + j
                        rec = P.emit("vector",
                                     lambda p, h_=hh_, hf=half, jj=j, i_=i,
                                     w_=w: p.scalar_tensor_tensor(
                                         out=xt[w_][:, h_ * 128:
                                                    (h_ + 1) * 128],
                                         in0=ps_b[:, hf * 1024 + jj * 128:
                                                  hf * 1024 + (jj + 1) * 128],
                                         scalar=lrec[:, h_ * 16 + i_:
                                                     h_ * 16 + i_ + 1],
                                         in1=xt[w_][:, h_ * 128:
                                                    (h_ + 1) * 128],
                                         op0=ALU.mult, op1=ALU.add),
                                     waits=[("pe", pev)] if j == 0 else [])
                    P.add_inc(rec, "dv", 1)
                    guard_set(("tp", half), "dv")
                xld[("otin",)] = P.cnt["dv"]
                stx = P.dma("gpsimd", f"sgx{w}",
                            x2_s[b, i * 128:(i + 1) * 128, :],
                            xt[w][:], waits=[("dv", P.cnt["dv"])])
                dvv = ln_tile(xt[w][:], h2b[:], h2b[:])

                def dv_h2(j0, n):
                    return h2a[:, j0 * 128:(j0 + n) * 128].rearrange(
                        "p (a c) -> p a c", c=128)
                h2w = [("sth", xld.get(("h2a",), 0))]
                transpose_groups(h2b, None, EC, 0, dv_h2,
                                 pe_waits=[("dv", dvv)] + h2w)
                sth = P.dma("gpsimd", "sth",
                            h2_s[b, :, :, i * 128:(i + 1) * 128].rearrange(
                                "a p c -> p a c"),
                            h2a[:, :].rearrange("p (a c) -> p a c", c=128),
                            waits=[("dv", P.cnt["dv"])])
                xld[("h2a",)] = sth
                x_tile_consumed(w, [("dv", P.cnt["dv"]),
                                    ("ac", P.cnt["ac"]),
                                    (f"sgx{w}", stx)])
                w_pace(len(wsteps), 2.0)

        def ffn():
            wld = {}
            for sc in range(4):
                b, thalf = divmod(sc, 2)
                t0 = thalf * 1024
                ldh = P.dma("sync", "ldh", h2Tf[:, :].rearrange(
                    "p (ec t) -> p ec t", t=1024),
                    h2_s[b, :, :, t0:t0 + 1024].rearrange(
                        "e p t -> p e t"),
                    waits=[("dv", wld.get("fin_dv", 0)),
                           ("pe", wld.get("fin_pe", 0))])
                for oc in range(OC):
                    wb = oc % 2
                    lw = [("pe", wld.get(("wg", oc - 2), 0))]
                    ldg = P.dma("sync", f"ldg{wb}", wgt[wb][:, :].rearrange(
                        "p (a c) -> p a c", c=128),
                        wg_s[:, :, oc * 128:(oc + 1) * 128].rearrange(
                            "a p c -> p a c"), waits=lw)
                    ldu = P.dma("sync", f"ldu{wb}", wut[wb][:, :].rearrange(
                        "p (a c) -> p a c", c=128),
                        wu_s[:, :, oc * 128:(oc + 1) * 128].rearrange(
                            "a p c -> p a c"))
                    for tc in range(2):
                        sl = next_slot()
                        gslot = ps_f[:, sl * 1024:sl * 1024 + 512]
                        uslot = ps_f[:, sl * 1024 + 512:sl * 1024 + 1024]
                        silsl = sil[:, sl * 512:sl * 512 + 512]
                        pw = guard_wait(("sl", sl)) + \
                            [(f"ldg{wb}", ldg), (f"ldu{wb}", ldu),
                             ("ldh", ldh)]
                        for ec2 in range(EC):
                            P.emit("tensor", lambda p, e=ec2, t=tc, g_=gslot,
                                   w_=wb: p.matmul(
                                       out=g_,
                                       lhsT=wgt[w_][:, e * 128:
                                                    (e + 1) * 128],
                                       rhs=h2Tf[:, e * 1024 + t * 512:
                                                e * 1024 + t * 512 + 512],
                                       start=(e == 0), stop=(e == EC - 1)),
                                   waits=pw if ec2 == 0 else [])
                        for ec2 in range(EC):
                            rec = P.emit("tensor",
                                         lambda p, e=ec2, t=tc, u_=uslot,
                                         w_=wb: p.matmul(
                                             out=u_,
                                             lhsT=wut[w_][:, e * 128:
                                                          (e + 1) * 128],
                                             rhs=h2Tf[:, e * 1024 + t * 512:
                                                      e * 1024 + t * 512
                                                      + 512],
                                             start=(e == 0),
                                             stop=(e == EC - 1)))
                        P.add_inc(rec, "pe", 1)
                        if tc == 1:
                            wld[("wg", oc)] = P.cnt["pe"]
                        pev = P.cnt["pe"]
                        rec = P.emit("scalar", lambda p, g_=gslot,
                                     s_=silsl: p.activation(
                                         out=s_, in_=g_, func=AF.Silu,
                                         bias=0.0, scale=1.0),
                                     waits=[("pe", pev)])
                        P.add_inc(rec, "ac", 1)
                        acv = P.cnt["ac"]
                        rec = P.emit("vector",
                                     lambda p, oc_=oc, t=tc, u_=uslot,
                                     s_=silsl: p.tensor_tensor(
                                         out=hh[:, oc_ * 1024 + t * 512:
                                                oc_ * 1024 + t * 512 + 512],
                                         in0=s_, in1=u_, op=ALU.mult),
                                     waits=[("ac", acv)] +
                                     ([("pe", wld.get("hh_pe", 0))]
                                      if oc == 0 and tc == 0 else []))
                        P.add_inc(rec, "dv", 1)
                        guard_set(("sl", sl), "dv")
                    w_pace(len(wsteps), 2.0)
                hhdone = P.cnt["dv"]
                for hc in range(EC):
                    for half in range(2):
                        oc0 = 0 if half == 0 else 22
                        ntl = 22 if half == 0 else 21
                        assert wS["t"] >= len(wsteps), "wd not materialized"
                        wld[("wd", hc, half)] = P.dma(
                            "sync", f"ldd{half}",
                            wdt[half][:, 0:ntl * 128].rearrange(
                                "p (a c) -> p a c", c=128),
                            wd_s[oc0:oc0 + ntl, :,
                                 hc * 128:(hc + 1) * 128].rearrange(
                                     "a p c -> p a c"),
                            waits=[("pe", wld.get(("wdu", hc - 1, half), 0)),
                                   ("sta0", P.cnt["sta0"]),
                                   ("sta1", P.cnt["sta1"])])
                    for tc in range(2):
                        sl = next_slot()
                        base = sl * 1024
                        pw = guard_wait(("sl", sl)) + \
                            [("dv", hhdone),
                             ("ldd0", wld[("wd", hc, 0)]),
                             ("ldd1", wld[("wd", hc, 1)])]
                        for oc in range(OC):
                            hf, rel = (0, oc) if oc < 22 else (1, oc - 22)
                            rec = P.emit("tensor",
                                         lambda p, o=oc, h_=hf, r=rel, t=tc,
                                         b_=base: p.matmul(
                                             out=ps_f[:, b_:b_ + 512],
                                             lhsT=wdt[h_][:, r * 128:
                                                          (r + 1) * 128],
                                             rhs=hh[:, o * 1024 + t * 512:
                                                    o * 1024 + t * 512
                                                    + 512],
                                             start=(o == 0),
                                             stop=(o == OC - 1)),
                                         waits=pw if oc == 0 else [])
                        P.add_inc(rec, "pe", 1)
                        if tc == 1:
                            wld[("wdu", hc, 0)] = P.cnt["pe"]
                            wld[("wdu", hc, 1)] = P.cnt["pe"]
                        pev = P.cnt["pe"]
                        rec = P.emit("scalar", lambda p, hc_=hc, t=tc,
                                     b_=base: p.activation(
                                         out=dna[:, hc_ * 1024 + t * 512:
                                                 hc_ * 1024 + t * 512 + 512],
                                         in_=ps_f[:, b_:b_ + 512],
                                         func=AF.Copy, bias=0.0, scale=1.0),
                                     waits=[("pe", pev)])
                        P.add_inc(rec, "ac", 1)
                        guard_set(("sl", sl), "ac")
                wld["hh_pe"] = P.cnt["pe"]
                dndone = P.cnt["ac"]
                for tt in range(8):
                    w = tt % 2
                    ldx = P.dma("sync", f"ldx{w}", xt[w][:],
                                x2_s[b, t0 + tt * 128:t0 + (tt + 1) * 128, :],
                                waits=xld.get(("fxt", w), []))
                    for j0 in range(0, EC, 4):
                        half = (j0 // 4) % 2
                        pw = guard_wait(("tp", half)) + \
                            ([("ac", dndone)] if j0 == 0 else [])
                        for j in range(4):
                            rec = P.emit("tensor",
                                         lambda p, hc=j0 + j, hf=half, jj=j,
                                         tt_=tt: p.transpose(
                                             out=ps_b[:, hf * 1024 +
                                                      jj * 128:
                                                      hf * 1024 +
                                                      (jj + 1) * 128],
                                             in_=dna[:, hc * 1024 +
                                                     tt_ * 128:
                                                     hc * 1024 + tt_ * 128
                                                     + 128],
                                             identity=ident[:]),
                                         waits=pw if j == 0 else
                                         ([(f"ldx{w}", ldx)] if j == 1 and
                                          j0 == 0 else []))
                        P.add_inc(rec, "pe", 1)
                        pev = P.cnt["pe"]
                        rec = P.emit("vector", lambda p, j0_=j0, hf=half,
                                     w_=w: p.tensor_tensor(
                                         out=xt[w_][:, j0_ * 128:
                                                    (j0_ + 4) * 128]
                                         .rearrange("p (a c) -> p a c",
                                                    c=128),
                                         in0=ps_b[:, hf * 1024:
                                                  hf * 1024 + 512]
                                         .rearrange("p (a c) -> p a c",
                                                    c=128),
                                         in1=xt[w_][:, j0_ * 128:
                                                    (j0_ + 4) * 128]
                                         .rearrange("p (a c) -> p a c",
                                                    c=128),
                                         op=ALU.add),
                                     waits=[("pe", pev)])
                        P.add_inc(rec, "dv", 1)
                        guard_set(("tp", half), "dv")
                    sty = P.dma("gpsimd", f"sgx{w}",
                                y_d[b, t0 + tt * 128:t0 + (tt + 1) * 128, :],
                                xt[w][:], waits=[("dv", P.cnt["dv"])])
                    xld[("fxt", w)] = [(f"sgx{w}", sty)]
                wld["fin_dv"] = P.cnt["dv"]
                wld["fin_pe"] = P.cnt["pe"]

        def p1tap(b):
            if debug_outputs and b == 0:
                P.dma("gpsimd", "sto", t_hTf, hT[:, :],
                      waits=[("dv", P.cnt["dv"])])
                P.dma("gpsimd", "sto", t_s1, dbg4[:, 0:16])
                P.dma("gpsimd", "sto", t_ss, dbg4[:, 16:32])
                P.dma("gpsimd", "sto", t_var, dbg4[:, 32:48])
                P.dma("gpsimd", "sto", t_rstd, dbg4[:, 48:64])
        phases = [("P0", None)]
        for b in range(BPC):
            phases += [(f"P1b{b}",
                        lambda b=b: (p1(b), p1tap(b), P.barrier())),
                       (f"ATTNb{b}", lambda b=b: (attn(b), P.barrier())),
                       (f"RESb{b}", lambda b=b: (res(b), P.barrier()))]
        def wflush():
            P.barrier()
        phases.append(("WFLUSH", wflush))
        phases.append(("FFN", ffn))
        for name, fn in phases:
            if fn is not None:
                fn()
            if stop_after is not None and name == stop_after:
                break
        P.emit("gpsimd", None,
               waits=[(s_, P.cnt[s_]) for s_ in
                      ("sta0", "sta1", "sto", "sgx0", "sgx1", "sth")
                      if P.cnt[s_] > 0])

        with nc.Block() as block:
            def runner(name):
                def _run(proxy):
                    for build_fn, waits, incs in P.ops[name]:
                        for s, v in waits:
                            if v > 0:
                                proxy.wait_ge(sems[s], v)
                        if build_fn is not None:
                            ins = build_fn(proxy)
                            for s, n in incs:
                                ins.then_inc(sems[s], n)
                return _run
            block.sync(runner("sync"))
            block.vector(runner("vector"))
            block.scalar(runner("scalar"))
            block.tensor(runner("tensor"))
            block.gpsimd(runner("gpsimd"))

    return nc


def kernel(L0_attn_x, L0_attn_Wq, L0_attn_bq, L0_attn_Wk, L0_attn_bk,
           L0_attn_Wv, L0_attn_bv, L0_ffn_lut, L0_ffn_gate_idx,
           L0_ffn_up_idx, L0_ffn_down_idx):
    x = np.ascontiguousarray(np.asarray(L0_attn_x, dtype=np.float32))
    wq = np.ascontiguousarray(np.asarray(L0_attn_Wq, dtype=np.float32))
    bq = np.ascontiguousarray(np.asarray(L0_attn_bq, dtype=np.float32))
    wk = np.ascontiguousarray(np.asarray(L0_attn_Wk, dtype=np.float32))
    bk = np.ascontiguousarray(np.asarray(L0_attn_bk, dtype=np.float32))
    wv = np.ascontiguousarray(np.asarray(L0_attn_Wv, dtype=np.float32))
    bv = np.ascontiguousarray(np.asarray(L0_attn_bv, dtype=np.float32))
    lut = np.asarray(L0_ffn_lut, dtype=np.float32)
    gidxT = np.ascontiguousarray(np.asarray(L0_ffn_gate_idx).T.astype(np.int8))
    uidxT = np.ascontiguousarray(np.asarray(L0_ffn_up_idx).T.astype(np.int8))
    didxT = np.ascontiguousarray(np.asarray(L0_ffn_down_idx).T.astype(np.int8))

    ident = np.eye(128, dtype=ml_dtypes.bfloat16)
    qi = np.arange(128)
    maskq = np.where(qi[None, :] <= qi[:, None], 0.0,
                     NEG).astype(ml_dtypes.bfloat16)
    maskt = np.ascontiguousarray(maskq.T)

    lutvals = [float(v) for v in lut]
    nc = bass.Bass("TRN2", target_bir_lowering=False,
                   detect_race_conditions=False)
    build(nc, lutvals)

    in_maps = []
    for c in range(NCORES):
        in_maps.append({
            "x": x[c * BPC:(c + 1) * BPC],
            "wq": wq, "bq": bq, "wk": wk, "bk": bk, "wv": wv, "bv": bv,
            "gidxT": gidxT, "uidxT": uidxT, "didxT": didxT,
            "ident": ident, "maskq": maskq, "maskt": maskt,
        })
    res = run_bass_kernel_spmd(nc, in_maps, core_ids=list(range(NCORES)))
    out = np.concatenate([res.results[c]["y"] for c in range(NCORES)],
                         axis=0)
    return out.astype(np.float32)
